# version 09
# baseline (speedup 1.0000x reference)
import contextlib
import numpy as np
import ml_dtypes
import concourse.bass as bass
import concourse.mybir as mybir
from concourse.bass_utils import run_bass_kernel_spmd
from concourse.alu_op_type import AluOpType as OP

F32 = mybir.dt.float32
BF16 = mybir.dt.bfloat16
AF = mybir.ActivationFunctionType
BF = ml_dtypes.bfloat16
f32 = np.float32

B_PER_CORE = 2
T = 2048
C = 1024
NG = 8
I_FF = 2816
NT = T // 128
NCH = T // 512
I_CHUNKS = [256] * 11
PASSES3 = [(0, 0), (0, 1), (1, 0)]
SWAP_MASK = [i ^ 1 for i in range(32)]
A1_WAVES = [("q", 0), ("k", C), ("v", 2 * C)]
PREP_PARTS = ("qkv", "proj", "wg", "wu", "wd")
WG_MODE = "full"
DEBUG = False
PIPE_A1 = True
PIPE_A2 = True
PIPE_A3 = True
PIPE_B = True

AFR_W = 8704
BV_COL = 6656
BP_COL = 7680
TC_COS = 0
TC_SIN = 2048
TC_BQ = 4096
TC_BK = 4104
TC_ST = 4112
TC_DEN = 4120
TAB_W = 4640


def _host_tables():
    import jax
    import jax.numpy as jnp
    t = {}
    with jax.default_device(jax.devices("cpu")[0]):
        pos = jnp.arange(T, dtype=jnp.float32)[:, None]
        i = jnp.arange(1, 33, dtype=jnp.float32)[None]
        theta = 1.0 / 10000.0 ** (2.0 * (i - 1.0) / 64.0)
        ang = pos * theta
        cosv = np.asarray(jnp.cos(ang), dtype=np.float64)
        sinv = np.asarray(jnp.sin(ang), dtype=np.float64)
    p = np.arange(128)
    jrow = (p % 64) // 2
    sgn = np.where(p % 2 == 0, -1.0, 1.0)
    cosT = cosv[:, jrow].T
    sinT = sinv[:, jrow].T * sgn[:, None]
    t["cos_h"] = cosT.astype(f32)
    t["cos_l"] = np.zeros_like(t["cos_h"])
    t["sin_h"] = sinT.astype(f32)
    t["sin_l"] = np.zeros_like(t["sin_h"])
    m = np.zeros((4, 128, 512), dtype=f32)
    for jj in range(4):
        kt = 128 * jj + np.arange(128)[:, None]
        q = np.arange(512)[None, :]
        m[jj] = np.where(kt > q, -1e9, 0.0)
    t["masks"] = m.astype(BF)
    t["ident"] = np.eye(128, dtype=BF)
    return t


def _host_bias_tables(b_qkv, b_proj):
    t = {}
    p = np.arange(128)
    bq = np.zeros((128, NG), dtype=f32)
    bk = np.zeros((128, NG), dtype=f32)
    for g in range(NG):
        head = 2 * g + p // 64
        d = p % 64
        bq[:, g] = b_qkv[192 * head + d]
        bk[:, g] = b_qkv[192 * head + 64 + d]
    t["bq_tab"] = bq
    t["bk_tab"] = bk
    jcol = np.arange(C)
    bv = b_qkv[192 * (jcol // 64) + 128 + (jcol % 64)]
    t["bv_bc"] = np.broadcast_to(bv.astype(f32), (128, C)).copy()
    t["bp_bc"] = np.broadcast_to(b_proj.astype(f32), (128, C)).copy()
    return t


class _EngP:
    _NOTRACK = {"wait_ge", "drain", "nop", "engine_nop", "sem_inc"}

    def __init__(self, eng, store, key):
        object.__setattr__(self, "_e", eng)
        object.__setattr__(self, "_store", store)
        object.__setattr__(self, "_key", key)

    @property
    def last(self):
        return self._store[self._key]

    def __getattr__(self, n):
        f = getattr(self._e, n)
        if callable(f) and not n.startswith("_"):
            track = n not in _EngP._NOTRACK
            store, key = self._store, self._key

            def w(*a, **k):
                r = f(*a, **k)
                if track and hasattr(r, "then_inc"):
                    store[key] = r
                return r
            return w
        return f


class Sem:
    def __init__(self, stack, nc, name):
        self.h = stack.enter_context(nc.semaphore(name))
        self.n = 0

    def inc(self, instr, k=1):
        instr.then_inc(self.h, k)
        self.n += k
        return self.n


class Builder:
    def __init__(self):
        self.nc = bass.Bass("TRN2", target_bir_lowering=False,
                            detect_race_conditions=False)
        nc = self.nc

        def dram(n, s, d, k=None):
            h = nc.dram_tensor(n, s, d, kind=k) if k else nc.dram_tensor(n, s, d)
            return h.ap()
        self.x = dram("x", [B_PER_CORE, T, C], F32, "ExternalInput")
        self.wqkv = dram("wqkv", [C, 3 * C], F32, "ExternalInput")
        self.wproj = dram("wproj", [C, C], F32, "ExternalInput")
        self.wg_in = dram("wg_in", [I_FF, C], F32, "ExternalInput")
        self.wu_in = dram("wu_in", [I_FF, C], F32, "ExternalInput")
        self.wd_in = dram("wd_in", [C, I_FF], F32, "ExternalInput")
        for n in ["cos_h", "cos_l", "sin_h", "sin_l"]:
            setattr(self, n, dram(n, [128, T], F32, "ExternalInput"))
        self.masks = dram("masks", [4, 128, 512], BF16, "ExternalInput")
        self.ident = dram("ident", [128, 128], BF16, "ExternalInput")
        self.bq_tab = dram("bq_tab", [128, NG], F32, "ExternalInput")
        self.bk_tab = dram("bk_tab", [128, NG], F32, "ExternalInput")
        self.bv_bc = dram("bv_bc", [128, C], F32, "ExternalInput")
        self.bp_bc = dram("bp_bc", [128, C], F32, "ExternalInput")
        self.out = dram("out", [B_PER_CORE, T, C], F32, "ExternalOutput")
        self.wqkv_s = dram("wqkv_s", [8, 2, 128, 3 * C], BF16)
        self.wproj_s = dram("wproj_s", [8, 2, 128, C], BF16)
        self.wg_s = dram("wg_s", [2, 8, 128, I_FF], BF16)
        self.wu_s = dram("wu_s", [2, 8, 128, I_FF], BF16)
        self.wd_s = dram("wd_s", [22, 2, 128, C], BF16)
        self.xn_s = dram("xn_s", [T, C], F32)
        self.kt_s = dram("kt_s", [NG, 2, 128, T], BF16)
        self.v_s = dram("v_s", [T, C], F32)
        self.part_s = dram("part_s", [T, C], F32)
        if DEBUG:
            self.dbg = dram("dbg", [3, 128, 16384], F32)
            self.dbg2 = dram("dbg2", [4, T, C], F32, "ExternalOutput")

        self.stack = contextlib.ExitStack()
        ec = self.stack.enter_context

        def sb(n, cols, dt):
            return ec(nc.sbuf_tensor(n, [128, cols], dt))
        self.AW = sb("AW", 12544, BF16)
        self.AT = sb("AT", 16384, BF16)
        self.AXNT = sb("AXNT", 8192, BF16)
        self.AACC = sb("AACC", 8192, F32)
        self.AFr = sb("AF", AFR_W, F32)
        self.AQ = sb("AQ", 5664, BF16)
        self.TAB = sb("TAB", TAB_W, F32)
        self.MSK = sb("MSK", 2048, BF16)
        self.KBUF = sb("KBUF", 1024, BF16)
        self.VBUF = sb("VBUF", 512, F32)
        self.ID16 = sb("ID16", 128, BF16)
        self.ID32 = sb("ID32", 128, F32)
        self.ONES16 = sb("ONES16", 16, BF16)
        self.ONESF = sb("ONESF", 64, F32)
        self.ps = [ec(nc.psum_tensor(f"pb{i}", [128, 512], F32)) for i in range(6)]
        self._psum_scoped = []

        self.s_bar = Sem(self.stack, nc, "s_bar")
        self.s_ld = Sem(self.stack, nc, "s_ld")
        self.s_st = Sem(self.stack, nc, "s_st")
        self.s_t = Sem(self.stack, nc, "s_t")
        self.s_v = Sem(self.stack, nc, "s_v")
        self.s_s = Sem(self.stack, nc, "s_s")
        self.block = ec(nc.Block())

    def on(self, eng_name, fn):
        if not hasattr(self, "_lasts"):
            self._lasts = {}
        self._lasts.setdefault(eng_name, None)

        def wrapped(eng):
            fn(_EngP(eng, self._lasts, eng_name))
        getattr(self.block, eng_name)(wrapped)

    def psum_open(self, name, cols, dt):
        self._psn = getattr(self, "_psn", 0) + 1
        g = self.nc.psum_tensor(f"{name}{self._psn}", [128, cols], dt)
        t = g.__enter__()
        self._psum_scoped.append(g)
        return t

    def psum_close(self, n=1):
        for _ in range(n):
            g = self._psum_scoped.pop()
            g.__exit__(None, None, None)

    def sentinel(self, eng):
        self.s_ld.inc(eng.dma_start(out=self.TAB[:, 4632:4640], in_=self.bq_tab), 16)

    def emit_tr_blocks(self, pairs, pre_wait=None):
        b = self
        t0 = b.s_t.n
        v0 = b.s_v.n
        banks = [self.trA, self.trB]

        def ften(eng):
            if pre_wait is not None:
                pre_wait(eng)
            for i, (s, d) in enumerate(pairs):
                if i >= 2:
                    eng.wait_ge(b.s_v.h, v0 + i - 1)
                b.s_t.inc(eng.transpose(out=banks[i % 2][:, :], in_=s,
                                        identity=b.ID16[:, :]))
        self.on("tensor", ften)

        def fvec(eng):
            for i, (s, d) in enumerate(pairs):
                eng.wait_ge(b.s_t.h, t0 + i + 1)
                b.s_v.inc(eng.tensor_copy(out=d, in_=banks[i % 2][:, :]))
        self.on("vector", fvec)

    def barrier(self):
        vals = [(self.s_t, self.s_t.n), (self.s_v, self.s_v.n),
                (self.s_s, self.s_s.n), (self.s_st, self.s_st.n),
                (self.s_ld, self.s_ld.n)]
        for e in ["sync", "gpsimd", "tensor", "vector", "scalar"]:
            def fn(eng, vals=vals):
                for s, n in vals:
                    eng.wait_ge(s.h, n)
            self.on(e, fn)

    def emit_setup(self):
        b = self
        TAB, AFr = b.TAB, b.AFr

        def fsync(eng):
            for src, dst in [(b.cos_h, AFr[:, 0:2048]), (b.cos_l, AFr[:, 2048:4096]),
                             (b.sin_h, AFr[:, 4096:6144]), (b.sin_l, AFr[:, 6144:8192])]:
                b.s_ld.inc(eng.dma_start(out=dst, in_=src), 16)
            b.s_ld.inc(eng.dma_start(out=TAB[:, TC_BQ:TC_BQ + NG], in_=b.bq_tab), 16)
            b.s_ld.inc(eng.dma_start(out=TAB[:, TC_BK:TC_BK + NG], in_=b.bk_tab), 16)
            b.s_ld.inc(eng.dma_start(
                out=b.MSK[:, 0:2048].rearrange("p (m q) -> p m q", m=4),
                in_=b.masks.transpose([1, 0, 2])), 16)
            b.s_ld.inc(eng.dma_start(out=b.ID16[:, :], in_=b.ident), 16)
        self.on("sync", fsync)
        ldv = b.s_ld.n

        def fvec(eng, ldv=ldv):
            eng.wait_ge(b.s_ld.h, ldv)
            eng.tensor_tensor(out=TAB[:, TC_COS:TC_COS + 2048],
                              in0=AFr[:, 0:2048], in1=AFr[:, 2048:4096], op=OP.add)
            eng.tensor_tensor(out=TAB[:, TC_SIN:TC_SIN + 2048],
                              in0=AFr[:, 4096:6144], in1=AFr[:, 6144:8192], op=OP.add)
            eng.memset(b.ONES16[:, :], 1.0)
            eng.memset(b.ONESF[:, :], 1.0)
            eng.tensor_copy(out=b.ID32[:, :], in_=b.ID16[:, :])
            b.s_v.inc(eng.last)
        self.on("vector", fvec)

    def emit_quant(self, F, x_ap, tcol, scol=5640, hi_ap=None, lo_ap=None):
        b = self
        A = b.AFr
        nb = F // 16
        S = A[:, scol:scol + nb]
        R = A[:, scol + nb:scol + 2 * nb]
        t0 = A[:, tcol:tcol + F]
        t1 = A[:, tcol + F:tcol + 2 * F]
        t2 = A[:, tcol + 2 * F:tcol + 3 * F]
        t3 = A[:, tcol + 3 * F:tcol + 4 * F]

        def fn(eng):
            eng.tensor_reduce(out=S.rearrange("p (n o) -> p n o", o=1),
                              in_=x_ap.rearrange("p (n s) -> p n s", s=16),
                              axis=mybir.AxisListType.X, op=OP.max,
                              apply_absolute_value=True)
            eng.drain()
            eng.tensor_scalar(out=S, in0=S, scalar1=1.0 / 6.0, scalar2=1e-12,
                              op0=OP.mult, op1=OP.max)
            eng.drain()
            eng.reciprocal(out=R, in_=S)
            eng.drain()
            Rb = R.rearrange("p (n o) -> p n o", o=1).broadcast_to((128, nb, 16))
            Sb = S.rearrange("p (n o) -> p n o", o=1).broadcast_to((128, nb, 16))
            x3 = x_ap.rearrange("p (n s) -> p n s", s=16)
            eng.tensor_tensor(out=t0.rearrange("p (n s) -> p n s", s=16),
                              in0=x3, in1=Rb, op=OP.mult)
            eng.scalar_tensor_tensor(out=t1, in0=t0, scalar=-1.0, in1=t0,
                                     op0=OP.mult, op1=OP.max)
            eng.tensor_scalar(out=t2, in0=t1, scalar1=0.25, scalar2=None, op0=OP.is_gt)
            for thr in (0.75, 1.25, 1.75):
                eng.scalar_tensor_tensor(out=t2, in0=t1, scalar=thr, in1=t2,
                                         op0=OP.is_gt, op1=OP.add)
            eng.tensor_scalar(out=t3, in0=t1, scalar1=2.5, scalar2=None, op0=OP.is_gt)
            eng.scalar_tensor_tensor(out=t3, in0=t1, scalar=3.5, in1=t3,
                                     op0=OP.is_gt, op1=OP.add)
            eng.tensor_scalar(out=t1, in0=t1, scalar1=5.0, scalar2=None, op0=OP.is_gt)
            eng.scalar_tensor_tensor(out=t3, in0=t1, scalar=2.0, in1=t3,
                                     op0=OP.mult, op1=OP.add)
            eng.scalar_tensor_tensor(out=t2, in0=t2, scalar=0.5, in1=t3,
                                     op0=OP.mult, op1=OP.add)
            eng.tensor_scalar(out=t1, in0=t0, scalar1=0.0, scalar2=None, op0=OP.is_lt)
            eng.tensor_tensor(out=t1, in0=t2, in1=t1, op=OP.mult)
            eng.scalar_tensor_tensor(out=t2, in0=t1, scalar=-2.0, in1=t2,
                                     op0=OP.mult, op1=OP.add)
            eng.tensor_tensor(out=x_ap.rearrange("p (n s) -> p n s", s=16),
                              in0=t2.rearrange("p (n s) -> p n s", s=16),
                              in1=Sb, op=OP.mult)
            if hi_ap is not None:
                eng.tensor_copy(out=hi_ap, in_=x_ap)
                eng.tensor_tensor(out=lo_ap, in0=x_ap, in1=hi_ap, op=OP.subtract)
        return fn

    def emit_weight_prep(self):
        b = self
        AFr, AW, AQ = b.AFr, b.AW, b.AQ
        for name, w_in, w_s, width in [("qkv", b.wqkv, b.wqkv_s, 3 * C),
                                       ("proj", b.wproj, b.wproj_s, C)]:
            if name not in PREP_PARTS:
                continue
            sec_v0, sec_st0 = b.s_v.n, b.s_st.n
            fv_n, st_n = [], []
            for kk in range(8):
                par = kk % 2
                src = AFr[:, par * 3072:par * 3072 + width]
                hi = AW[:, par * 3072:par * 3072 + width]
                lo = AW[:, 6272 + par * 3072:6272 + par * 3072 + width]

                def fsync(eng, kk=kk, w_in=w_in, src=src,
                          wv=(fv_n[kk - 2] if kk >= 2 else sec_v0)):
                    eng.wait_ge(b.s_v.h, wv)
                    b.s_ld.inc(eng.dma_start(out=src,
                                             in_=w_in[kk * 128:(kk + 1) * 128, :]), 16)
                    b.sentinel(eng)
                self.on("sync", fsync)
                ldv = b.s_ld.n

                def fvec(eng, ldv=ldv, src=src, hi=hi, lo=lo,
                         ws=(st_n[kk - 2] if kk >= 2 else sec_st0)):
                    eng.wait_ge(b.s_ld.h, ldv)
                    eng.wait_ge(b.s_st.h, ws)
                    eng.tensor_copy(out=hi, in_=src)
                    eng.tensor_tensor(out=lo, in0=src, in1=hi, op=OP.subtract)
                    b.s_v.inc(eng.last)
                self.on("vector", fvec)
                fv_n.append(b.s_v.n)
                vv = b.s_v.n

                def fst(eng, kk=kk, name=name, w_s=w_s, width=width, vv=vv,
                        hi=hi, lo=lo):
                    eng.wait_ge(b.s_v.h, vv)
                    for ihl, buf in [(0, hi), (1, lo)]:
                        if name == "qkv":
                            for sect in range(3):
                                src_ = buf.rearrange(
                                    "p (h s d) -> p h s d", h=16, s=3)[:, :, sect, :]
                                dst = w_s[kk, ihl, :, sect * C:(sect + 1) * C].rearrange(
                                    "p (h d) -> p h d", h=16)
                                b.s_st.inc(eng.dma_start(out=dst, in_=src_), 16)
                        else:
                            b.s_st.inc(eng.dma_start(out=w_s[kk, ihl, :, :],
                                                     in_=buf), 16)
                self.on("gpsimd", fst)
                st_n.append(b.s_st.n)
        self.trA = self.psum_open("trA", 128, BF16)
        self.trB = self.psum_open("trB", 128, BF16)
        for wname, w_in, w_s in [("wg", b.wg_in, b.wg_s), ("wu", b.wu_in, b.wu_s)]:
            if wname not in PREP_PARTS:
                continue
            sec_v0, sec_st0, sec_t0 = b.s_v.n, b.s_st.n, b.s_t.n
            qv_n, tv_n, st_n = [], [], []
            for it in range(22):
                par = it % 2
                src = AFr[:, 5824:5824 + C] if par == 0 else AFr[:, 4608:4608 + C]
                hl = AQ if par == 0 else AW
                hi, lo = hl[:, 0:C], hl[:, 1024:2048]
                trn = hl[:, 2048:4096]

                def fsync(eng, w_in=w_in, it=it, src=src,
                          wv=(qv_n[it - 2] if it >= 2 else sec_v0)):
                    eng.wait_ge(b.s_v.h, wv)
                    b.s_ld.inc(eng.dma_start(out=src,
                                             in_=w_in[it * 128:(it + 1) * 128, :]), 16)
                    b.sentinel(eng)
                self.on("sync", fsync)
                ldv = b.s_ld.n

                def fvw(eng, ldv=ldv, it=it,
                        wt=(tv_n[it - 2] if it >= 2 else sec_t0)):
                    eng.wait_ge(b.s_ld.h, ldv)
                    eng.wait_ge(b.s_t.h, wt)
                    if it < 2:
                        eng.wait_ge(b.s_st.h, sec_st0)
                self.on("vector", fvw)
                self.on("vector", self.emit_quant(C, src, tcol=0,
                                                  hi_ap=hi, lo_ap=lo))

                def fv2(eng):
                    b.s_v.inc(eng.last)
                self.on("vector", fv2)
                qv_n.append(b.s_v.n)
                vv = b.s_v.n

                pairs = []
                for ihl in range(2):
                    for kk in range(8):
                        pairs.append((hl[:, ihl * 1024 + kk * 128:ihl * 1024 + (kk + 1) * 128],
                                      trn[:, kk * 128 + ihl * 1024:
                                           kk * 128 + ihl * 1024 + 128]))
                ws = st_n[it - 2] if it >= 2 else sec_st0
                self.emit_tr_blocks(
                    pairs, pre_wait=lambda eng, vv=vv, ws=ws: (
                        eng.wait_ge(b.s_v.h, vv), eng.wait_ge(b.s_st.h, ws)))
                tv_n.append(b.s_t.n)
                vv2 = b.s_v.n

                def fst(eng, it=it, w_s=w_s, vv2=vv2, trn=trn):
                    eng.wait_ge(b.s_v.h, vv2)
                    for ihl in range(2):
                        dst = w_s[ihl, :, :, it * 128:(it + 1) * 128].transpose([1, 0, 2])
                        src_ = trn[:, ihl * 1024:(ihl + 1) * 1024].rearrange(
                            "p (k q) -> p k q", k=8)
                        b.s_st.inc(eng.dma_start(out=dst, in_=src_), 16)
                self.on("gpsimd", fst)
                st_n.append(b.s_st.n)
        sec_v0, sec_st0, sec_t0 = b.s_v.n, b.s_st.n, b.s_t.n
        qv_n, tv_n, st_n = [], [], []
        for ht in (range(8) if "wd" in PREP_PARTS else []):
            par = ht % 2
            sA, sc = (AFr, 5824) if par == 0 else (b.AACC, 0)
            src = sA[:, sc:sc + I_FF]

            def fsync(eng, ht=ht, src=src,
                      wv=(qv_n[ht - 2] if ht >= 2 else sec_v0)):
                eng.wait_ge(b.s_v.h, wv)
                b.s_ld.inc(eng.dma_start(out=src,
                                         in_=b.wd_in[ht * 128:(ht + 1) * 128, :]), 16)
                b.sentinel(eng)
            self.on("sync", fsync)
            ldv = b.s_ld.n

            def fvw(eng, ldv=ldv, ht=ht,
                    wt=(tv_n[ht - 1] if ht >= 1 else sec_t0)):
                eng.wait_ge(b.s_ld.h, ldv)
                eng.wait_ge(b.s_t.h, wt)
                if ht < 1:
                    eng.wait_ge(b.s_st.h, sec_st0)
            self.on("vector", fvw)
            for half in range(2):
                self.on("vector", self.emit_quant(
                    1408, sA[:, sc + half * 1408:sc + half * 1408 + 1408], tcol=0,
                    hi_ap=AQ[:, half * 1408:half * 1408 + 1408],
                    lo_ap=AQ[:, 2816 + half * 1408:2816 + half * 1408 + 1408]))

            def fv2(eng):
                b.s_v.inc(eng.last)
            self.on("vector", fv2)
            qv_n.append(b.s_v.n)
            vvq = b.s_v.n
            pairs = []
            for ihl in range(2):
                for s in range(22):
                    pairs.append((AQ[:, ihl * 2816 + s * 128:ihl * 2816 + (s + 1) * 128],
                                  b.AW[:, ihl * 2816 + s * 128:ihl * 2816 + (s + 1) * 128]))
            ws = st_n[ht - 1] if ht >= 1 else sec_st0
            self.emit_tr_blocks(
                pairs, pre_wait=lambda eng, vvq=vvq, ws=ws: (
                    eng.wait_ge(b.s_v.h, vvq), eng.wait_ge(b.s_st.h, ws)))
            tv_n.append(b.s_t.n)
            vv2 = b.s_v.n

            def fst(eng, ht=ht, vv2=vv2):
                eng.wait_ge(b.s_v.h, vv2)
                for ihl in range(2):
                    dst = b.wd_s[:, ihl, :, ht * 128:(ht + 1) * 128].transpose([1, 0, 2])
                    src = b.AW[:, ihl * 2816:ihl * 2816 + 2816].rearrange(
                        "p (k q) -> p k q", k=22)
                    b.s_st.inc(eng.dma_start(out=dst, in_=src), 16)
            self.on("gpsimd", fst)
            st_n.append(b.s_st.n)
        self.psum_close(2)
        self.barrier()

    def emit_bias_setup(self):
        b = self

        def fsync(eng):
            b.s_ld.inc(eng.dma_start(out=b.AFr[:, BV_COL:BV_COL + C], in_=b.bv_bc), 16)
            b.s_ld.inc(eng.dma_start(out=b.AFr[:, BP_COL:BP_COL + C], in_=b.bp_bc), 16)
        self.on("sync", fsync)

    def _a1(self, bi, c):
        b = self
        AFr, AT, AXNT, AACC, AW, TAB, AQ = b.AFr, b.AT, b.AXNT, b.AACC, b.AW, b.TAB, b.AQ
        self.trA = self.psum_open("trA", 128, BF16)
        self.trB = self.psum_open("trB", 128, BF16)
        for j in range(4):
            t = 4 * c + j

            def fsync(eng, t=t):
                eng.wait_ge(b.s_v.h, b.s_v.n)
                b.s_ld.inc(eng.dma_start(out=AFr[:, 0:1024],
                                         in_=b.x[bi, t * 128:(t + 1) * 128, :]), 16)
                b.sentinel(eng)
            self.on("sync", fsync)
            ldv = b.s_ld.n

            def fscl(eng, ldv=ldv):
                eng.wait_ge(b.s_ld.h, ldv)
                eng.wait_ge(b.s_st.h, b.s_st.n)
                st = TAB[:, TC_ST:TC_ST + 1]
                st2 = TAB[:, TC_ST + 1:TC_ST + 2]
                eng.activation(out=AFr[:, 1024:2048], in_=AFr[:, 0:1024], func=AF.Copy,
                               accum_out=st)
                eng.activation(out=AFr[:, 1024:2048], in_=AFr[:, 0:1024], func=AF.Square,
                               accum_out=st2)
                b.s_s.inc(eng.last)
            self.on("scalar", fscl)
            sv = b.s_s.n

            def fvec(eng, sv=sv):
                eng.wait_ge(b.s_s.h, sv)
                st = TAB[:, TC_ST:TC_ST + 1]
                st2 = TAB[:, TC_ST + 1:TC_ST + 2]
                mu = TAB[:, TC_ST + 2:TC_ST + 3]
                var = TAB[:, TC_ST + 3:TC_ST + 4]
                eng.tensor_scalar(out=mu, in0=st, scalar1=1.0 / C, scalar2=None, op0=OP.mult)
                eng.tensor_scalar(out=var, in0=st2, scalar1=1.0 / C, scalar2=None, op0=OP.mult)
                eng.drain()
                eng.tensor_tensor(out=st, in0=mu, in1=mu, op=OP.mult)
                eng.drain()
                eng.tensor_tensor(out=var, in0=var, in1=st, op=OP.subtract)
                b.s_v.inc(eng.last)
            self.on("vector", fvec)
            vv = b.s_v.n

            def fscl2(eng, vv=vv):
                eng.wait_ge(b.s_v.h, vv)
                eng.activation(out=TAB[:, TC_ST + 4:TC_ST + 5],
                               in_=TAB[:, TC_ST + 3:TC_ST + 4], func=AF.Sqrt,
                               bias=np.float32(1e-5), scale=np.float32(1.0))
                b.s_s.inc(eng.last)
            self.on("scalar", fscl2)
            sv2 = b.s_s.n

            def fvec2(eng, sv2=sv2):
                eng.wait_ge(b.s_s.h, sv2)
                rs = TAB[:, TC_ST + 5:TC_ST + 6]
                mu = TAB[:, TC_ST + 2:TC_ST + 3]
                eng.reciprocal(out=rs, in_=TAB[:, TC_ST + 4:TC_ST + 5])
                eng.drain()
                eng.tensor_scalar(out=AFr[:, 1024:2048], in0=AFr[:, 0:1024],
                                  scalar1=mu[:, 0:1], scalar2=rs[:, 0:1],
                                  op0=OP.subtract, op1=OP.mult)
                eng.tensor_copy(out=AQ[:, 0:1024], in_=AFr[:, 1024:2048])
                eng.tensor_tensor(out=AQ[:, 1024:2048], in0=AFr[:, 1024:2048],
                                  in1=AQ[:, 0:1024], op=OP.subtract)
                b.s_v.inc(eng.last)
            self.on("vector", fvec2)
            vv2 = b.s_v.n

            def fst(eng, t=t, vv2=vv2):
                eng.wait_ge(b.s_v.h, vv2)
                b.s_st.inc(eng.dma_start(out=b.xn_s[t * 128:(t + 1) * 128, :],
                                         in_=AFr[:, 1024:2048]), 16)
            self.on("gpsimd", fst)

            pairs = []
            for ihl in range(2):
                for kk in range(8):
                    pairs.append((AQ[:, ihl * 1024 + kk * 128:ihl * 1024 + (kk + 1) * 128],
                                  AXNT[:, ihl * 4096 + kk * 512 + j * 128:
                                       ihl * 4096 + kk * 512 + (j + 1) * 128]))
            self.emit_tr_blocks(
                pairs, pre_wait=lambda eng, vv2=vv2: eng.wait_ge(b.s_v.h, max(vv2, b.s_v.n)))
        vvx = b.s_v.n
        self.psum_close(2)
        pse = [self.psum_open("pe6", 512, F32), self.psum_open("pe7", 512, F32)]
        bank8 = b.ps + pse

        def xnT(ihl, kk):
            return b.AXNT[:, ihl * 4096 + kk * 512:ihl * 4096 + (kk + 1) * 512]
        for wave, colbase in A1_WAVES:
            tv_kk = []
            wave_t0 = b.s_t.n
            for kk in range(8):
                def fsync(eng, kk=kk, colbase=colbase,
                          tneed=None):
                    slot = kk % 2
                    depth = 2 if PIPE_A1 else 1
                    eng.wait_ge(b.s_t.h, tv_kk[kk - depth] if kk >= depth else wave_t0)
                    for ihl in range(2):
                        b.s_ld.inc(eng.dma_start(
                            out=AW[:, (slot * 2 + ihl) * 1024:(slot * 2 + ihl + 1) * 1024],
                            in_=b.wqkv_s[kk, ihl, :, colbase:colbase + C]), 16)
                    b.sentinel(eng)
                self.on("sync", fsync)
                ldv = b.s_ld.n

                def ften(eng, kk=kk, wave=wave, ldv=ldv, vvx=vvx):
                    eng.wait_ge(b.s_ld.h, ldv)
                    eng.wait_ge(b.s_v.h, vvx)
                    if kk == 0:
                        eng.wait_ge(b.s_v.h, b.s_v.n)
                        eng.wait_ge(b.s_s.h, b.s_s.n)
                    slot = kk % 2

                    def wbuf(ihl):
                        return AW[:, (slot * 2 + ihl) * 1024:(slot * 2 + ihl + 1) * 1024]
                    last = None
                    if wave in ("q", "k"):
                        for g in range(NG):
                            for pi, (whl, xhl) in enumerate(PASSES3):
                                last = eng.matmul(out=bank8[g][:, :],
                                                  lhsT=wbuf(whl)[:, g * 128:(g + 1) * 128],
                                                  rhs=xnT(xhl, kk),
                                                  start=(kk == 0 and pi == 0),
                                                  stop=(kk == 7 and pi == 2))
                    else:
                        for j in range(4):
                            for pi, (whl, xhl) in enumerate(PASSES3):
                                for half in range(2):
                                    last = eng.matmul(
                                        out=bank8[j * 2 + half][:, :],
                                        lhsT=xnT(xhl, kk)[:, j * 128:(j + 1) * 128],
                                        rhs=wbuf(whl)[:, half * 512:(half + 1) * 512],
                                        start=(kk == 0 and pi == 0),
                                        stop=(kk == 7 and pi == 2))
                    b.s_t.inc(last)
                self.on("tensor", ften)
                tv_kk.append(b.s_t.n)
            tvw = b.s_t.n
            if wave in ("q", "k"):
                for g in range(NG):
                    def fscl(eng, g=g, wave=wave, tvw=tvw):
                        eng.wait_ge(b.s_t.h, tvw)
                        eng.wait_ge(b.s_v.h, b.s_v.n)
                        col = (TC_BQ if wave == "q" else TC_BK) + g
                        eng.activation(out=AFr[:, 2048:2560], in_=bank8[g][:, :],
                                       func=AF.Identity, bias=TAB[:, col:col + 1],
                                       scale=np.float32(1.0))
                        b.s_s.inc(eng.last)
                    self.on("scalar", fscl)
                    sv = b.s_s.n

                    def fvec(eng, g=g, c=c, wave=wave, sv=sv):
                        eng.wait_ge(b.s_s.h, sv)
                        q_ = AFr[:, 2048:2560]
                        sw = AFr[:, 2560:3072]
                        t1 = AFr[:, 3072:3584]
                        r32 = AFr[:, 3584:4096]
                        eng.stream_shuffle(out=sw, in_=q_, mask=SWAP_MASK)
                        eng.tensor_tensor(out=t1, in0=q_,
                                          in1=TAB[:, TC_COS + 512 * c:TC_COS + 512 * (c + 1)],
                                          op=OP.mult)
                        eng.tensor_tensor(out=sw, in0=sw,
                                          in1=TAB[:, TC_SIN + 512 * c:TC_SIN + 512 * (c + 1)],
                                          op=OP.mult)
                        eng.tensor_tensor(out=r32, in0=t1, in1=sw, op=OP.add)
                        base = 0 if wave == "q" else 8192
                        hi = AT[:, base + g * 512:base + (g + 1) * 512]
                        lo = AT[:, base + 4096 + g * 512:base + 4096 + (g + 1) * 512]
                        eng.tensor_copy(out=hi, in_=r32)
                        eng.tensor_tensor(out=lo, in0=r32, in1=hi, op=OP.subtract)
                        b.s_v.inc(eng.last)
                    self.on("vector", fvec)
                if wave == "k":
                    vvk = b.s_v.n

                    def fstk(eng, c=c, vvk=vvk):
                        eng.wait_ge(b.s_v.h, vvk)
                        for g in range(NG):
                            for ihl, base in [(0, 8192), (1, 8192 + 4096)]:
                                b.s_st.inc(eng.dma_start(
                                    out=b.kt_s[g, ihl, :, c * 512:(c + 1) * 512],
                                    in_=AT[:, base + g * 512:base + (g + 1) * 512]), 16)
                    self.on("gpsimd", fstk)
            else:
                for j in range(4):
                    def fvec(eng, j=j, tvw=tvw):
                        eng.wait_ge(b.s_t.h, tvw)
                        for half in range(2):
                            eng.tensor_tensor(
                                out=AACC[:, j * 1024 + half * 512:j * 1024 + (half + 1) * 512],
                                in0=bank8[j * 2 + half][:, :],
                                in1=AFr[:, BV_COL + half * 512:BV_COL + (half + 1) * 512],
                                op=OP.add)
                        b.s_v.inc(eng.last)
                    self.on("vector", fvec)
                vvv = b.s_v.n

                def fstv(eng, c=c, vvv=vvv):
                    eng.wait_ge(b.s_v.h, vvv)
                    dst = b.v_s[c * 512:(c + 1) * 512, :].rearrange(
                        "(j p) f -> j p f", p=128).transpose([1, 0, 2])
                    src = AACC[:, 0:4096].rearrange("p (j f) -> p j f", j=4)
                    b.s_st.inc(eng.dma_start(out=dst, in_=src), 16)
                self.on("gpsimd", fstv)
        self.psum_close(2)

    def _a2(self, bi, c):
        b = self
        AFr, AT, AXNT, TAB, AQ = b.AFr, b.AT, b.AXNT, b.TAB, b.AQ
        nr = 4 * c + 4
        pbc = self.psum_open("pbc", 512, F32)
        PT = [2048, 5632]

        def emit_av(g, r, tv_av, sv_r, vprev):
            rb = r % 2
            pb = PT[rb]
            sve = sv_r[r]

            def ften_av(eng, g=g, r=r, rb=rb, pb=pb, sve=sve,
                        first=(r == 0), last=(r == nr - 1), vprev=vprev):
                eng.wait_ge(b.s_s.h, sve)
                if first:
                    eng.wait_ge(b.s_v.h, vprev)
                vb = b.VBUF[:, rb * 128:(rb + 1) * 128]
                eng.matmul(out=b.ps[4][0:64, :], lhsT=vb[:, 0:64],
                           rhs=AFr[:, pb:pb + 512], start=first, stop=last,
                           tile_position=(0, 0), skip_group_check=True)
                eng.matmul(out=b.ps[4][64:128, :], lhsT=vb[:, 64:128],
                           rhs=AFr[:, pb + 512:pb + 1024], start=first, stop=last,
                           tile_position=(0, 64), skip_group_check=True)
                eng.matmul(out=b.ps[5][0:1, :], lhsT=b.ONESF[:, 0:1],
                           rhs=AFr[:, pb:pb + 512], start=first, stop=last,
                           tile_position=(0, 0), skip_group_check=True)
                b.s_t.inc(eng.matmul(out=b.ps[5][32:33, :], lhsT=b.ONESF[:, 1:2],
                           rhs=AFr[:, pb + 512:pb + 1024], start=first, stop=last,
                           tile_position=(0, 32), skip_group_check=True))
            self.on("tensor", ften_av)
            tv_av.append(b.s_t.n)

        for g in range(NG):
            if g == 1:
                self._a3_prefetch(bi, c)
            tv_av = []
            tv_s = []
            sv_r = []
            vprev = b.s_v.n
            for r in range(nr):
                rb = r % 2
                sb_bank = 2 * rb

                def fsync_r(eng, g=g, r=r, rb=rb):
                    eng.wait_ge(b.s_t.h, tv_av[r - 2] if r >= 2 else b.s_t.n)
                    for ihl in range(2):
                        b.s_ld.inc(eng.dma_start(
                            out=b.KBUF[:, rb * 512 + ihl * 128:rb * 512 + (ihl + 1) * 128],
                            in_=b.kt_s[g, ihl, :, r * 128:(r + 1) * 128]), 16)
                    b.s_ld.inc(eng.dma_start(
                        out=b.VBUF[:, rb * 128:(rb + 1) * 128],
                        in_=b.v_s[r * 128:(r + 1) * 128, 128 * g:128 * (g + 1)]), 16)
                self.on("sync", fsync_r)
                ldv = b.s_ld.n

                def ften_s(eng, g=g, r=r, rb=rb, sb_bank=sb_bank, ldv=ldv):
                    eng.wait_ge(b.s_ld.h, ldv)
                    if r >= 2:
                        eng.wait_ge(b.s_s.h, sv_r[r - 2])

                    def kb(ihl):
                        return b.KBUF[:, rb * 512 + ihl * 128:rb * 512 + (ihl + 1) * 128]
                    last = None
                    for pi, (khl, qhl) in enumerate(PASSES3):
                        qsl = AT[:, qhl * 4096 + g * 512:qhl * 4096 + (g + 1) * 512]
                        eng.matmul(out=b.ps[sb_bank][:, :], lhsT=kb(khl)[0:64, :],
                                   rhs=qsl[0:64, :], start=(pi == 0), stop=(pi == 2),
                                   tile_position=(0, 0))
                        last = eng.matmul(out=b.ps[sb_bank + 1][:, :], lhsT=kb(khl)[64:128, :],
                                          rhs=qsl[64:128, :], start=(pi == 0), stop=(pi == 2),
                                          tile_position=(64, 0))
                    b.s_t.inc(last)
                self.on("tensor", ften_s)
                tv_s.append(b.s_t.n)
                tvs = b.s_t.n

                jj = r - 4 * c
                mv = None
                if jj >= 0:
                    def fvec_m(eng, jj=jj, sb_bank=sb_bank, tvs=tvs):
                        eng.wait_ge(b.s_t.h, tvs)
                        w = (jj + 1) * 128
                        for bk in range(2):
                            eng.tensor_tensor(out=b.ps[sb_bank + bk][:, 0:w],
                                              in0=b.ps[sb_bank + bk][:, 0:w],
                                              in1=b.MSK[:, jj * 512:jj * 512 + w],
                                              op=OP.add)
                        b.s_v.inc(eng.last)
                    self.on("vector", fvec_m)
                    mv = b.s_v.n

                def fscl_e(eng, r=r, rb=rb, sb_bank=sb_bank, tvs=tvs, mv=mv):
                    eng.wait_ge(b.s_t.h, tvs)
                    if r >= 2:
                        eng.wait_ge(b.s_t.h, tv_av[r - 2])
                    if mv is not None:
                        eng.wait_ge(b.s_v.h, mv)
                    pb = PT[rb]
                    for bk in range(2):
                        eng.activation(out=AFr[:, pb + bk * 512:pb + (bk + 1) * 512],
                                       in_=b.ps[sb_bank + bk][:, :], func=AF.Exp,
                                       bias=np.float32(-4.0), scale=np.float32(0.125))
                    b.s_s.inc(eng.last)
                self.on("scalar", fscl_e)
                sv_r.append(b.s_s.n)
                if PIPE_A2:
                    if r >= 1:
                        emit_av(g, r - 1, tv_av, sv_r, vprev)
                else:
                    emit_av(g, r, tv_av, sv_r, vprev)
            if PIPE_A2:
                emit_av(g, nr - 1, tv_av, sv_r, vprev)
            tva = tv_av[-1]

            def fvec_d(eng, tva=tva):
                eng.wait_ge(b.s_t.h, tva)
                dn = TAB[:, TC_DEN:TC_DEN + 512]
                eng.tensor_copy(out=dn[0:1, :], in_=b.ps[5][0:1, :])
                eng.tensor_copy(out=dn[32:33, :], in_=b.ps[5][32:33, :])
                eng.reciprocal(out=dn[0:1, :], in_=dn[0:1, :])
                eng.reciprocal(out=dn[32:33, :], in_=dn[32:33, :])
                b.s_v.inc(eng.last)
            self.on("vector", fvec_d)
            vvd = b.s_v.n

            def ften_b(eng, vvd=vvd):
                eng.wait_ge(b.s_v.h, vvd)
                dn = TAB[:, TC_DEN:TC_DEN + 512]
                eng.matmul(out=pbc[0:64, :], lhsT=b.ONESF[0:1, 0:64],
                           rhs=dn[0:1, :], start=True, stop=True, tile_position=(0, 0))
                b.s_t.inc(eng.matmul(out=pbc[64:128, :], lhsT=b.ONESF[32:33, 0:64],
                           rhs=dn[32:33, :], start=True, stop=True,
                           tile_position=(32, 64), skip_group_check=True))
            self.on("tensor", ften_b)
            tvb = b.s_t.n

            def fvec_y(eng, g=g, tvb=tvb):
                eng.wait_ge(b.s_t.h, tvb)
                bc = AFr[:, 3072:3584]
                yn = AFr[:, 3584:4096]
                eng.tensor_copy(out=bc, in_=pbc[:, :])
                eng.tensor_tensor(out=yn, in0=b.ps[4][:, :], in1=bc, op=OP.mult)
                hi = AXNT[:, g * 512:(g + 1) * 512]
                lo = AXNT[:, 4096 + g * 512:4096 + (g + 1) * 512]
                eng.tensor_copy(out=hi, in_=yn)
                eng.tensor_tensor(out=lo, in0=yn, in1=hi, op=OP.subtract)
                b.s_v.inc(eng.last)
            self.on("vector", fvec_y)
        self.psum_close(1)

    def _a3_prefetch(self, bi, c):
        b = self

        def fsync0(eng, bi=bi, c=c):
            dst = b.AACC[:, 0:4096].rearrange("p (j f) -> p j f", j=4)
            src = b.x[bi, c * 512:(c + 1) * 512, :].rearrange(
                "(j p) f -> j p f", p=128).transpose([1, 0, 2])
            b.s_ld.inc(eng.dma_start(out=dst, in_=src), 16)
            b.sentinel(eng)
        self.on("sync", fsync0)
        self._a3_ldx = b.s_ld.n

    def _a3(self, bi, c):
        b = self
        AFr, AXNT, AACC, AW = b.AFr, b.AXNT, b.AACC, b.AW
        ldx = self._a3_ldx

        def fvec_x(eng, ldx=ldx):
            eng.wait_ge(b.s_ld.h, ldx)
            for j in range(4):
                eng.tensor_tensor(out=AACC[:, j * 1024:(j + 1) * 1024],
                                  in0=AACC[:, j * 1024:(j + 1) * 1024],
                                  in1=AFr[:, BP_COL:BP_COL + C], op=OP.add)
            b.s_v.inc(eng.last)
        self.on("vector", fvec_x)
        vvx = b.s_v.n
        pse = [self.psum_open("pa6", 512, F32), self.psum_open("pa7", 512, F32)]
        bank8 = b.ps + pse
        tv_g = []
        a3_t0 = b.s_t.n
        for g in range(NG):
            def fsync(eng, g=g):
                slot = g % 2
                depth = 2 if PIPE_A3 else 1
                eng.wait_ge(b.s_t.h, tv_g[g - depth] if g >= depth else a3_t0)
                for ihl in range(2):
                    b.s_ld.inc(eng.dma_start(
                        out=AW[:, (slot * 2 + ihl) * 1024:(slot * 2 + ihl + 1) * 1024],
                        in_=b.wproj_s[g, ihl, :, :]), 16)
                b.sentinel(eng)
            self.on("sync", fsync)
            ldv = b.s_ld.n

            def ften(eng, g=g, ldv=ldv, vvx=vvx):
                eng.wait_ge(b.s_ld.h, ldv)
                if g == 0:
                    eng.wait_ge(b.s_v.h, max(vvx, b.s_v.n))
                slot = g % 2
                last = None
                for j in range(4):
                    for half in range(2):
                        for pi, (whl, yhl) in enumerate(PASSES3):
                            last = eng.matmul(
                                out=bank8[j * 2 + half][:, :],
                                lhsT=AXNT[:, yhl * 4096 + g * 512 + j * 128:
                                          yhl * 4096 + g * 512 + (j + 1) * 128],
                                rhs=AW[:, (slot * 2 + whl) * 1024 + half * 512:
                                       (slot * 2 + whl) * 1024 + (half + 1) * 512],
                                start=(g == 0 and pi == 0),
                                stop=(g == NG - 1 and pi == 2))
                b.s_t.inc(last)
            self.on("tensor", ften)
            tv_g.append(b.s_t.n)
        tvp = b.s_t.n
        for j in range(4):
            def fvec(eng, j=j, tvp=tvp):
                eng.wait_ge(b.s_t.h, tvp)
                for half in range(2):
                    eng.tensor_tensor(
                        out=AACC[:, 4096 + j * 1024 + half * 512:
                                 4096 + j * 1024 + (half + 1) * 512],
                        in0=bank8[j * 2 + half][:, :],
                        in1=AACC[:, j * 1024 + half * 512:j * 1024 + (half + 1) * 512],
                        op=OP.add)
                b.s_v.inc(eng.last)
            self.on("vector", fvec)
        vvp = b.s_v.n

        def fst(eng, c=c, vvp=vvp):
            eng.wait_ge(b.s_v.h, vvp)
            dst = b.part_s[c * 512:(c + 1) * 512, :].rearrange(
                "(j p) f -> j p f", p=128).transpose([1, 0, 2])
            src = AACC[:, 4096:8192].rearrange("p (j f) -> p j f", j=4)
            b.s_st.inc(eng.dma_start(out=dst, in_=src), 16)
        self.on("gpsimd", fst)
        self.psum_close(2)

    def emit_phase_b(self, bi):
        b = self
        AFr, AT, AXNT, AACC, AW, AQ = b.AFr, b.AT, b.AXNT, b.AACC, b.AW, b.AQ
        self.trA = self.psum_open("trA", 128, BF16)
        self.trB = self.psum_open("trB", 128, BF16)
        for half in range(2):
            toff = half * 8

            def fsync_p(eng, toff=toff):
                dst = AACC[:, 0:8192].rearrange("p (j f) -> p j f", j=8)
                src = b.part_s[toff * 128:(toff + 8) * 128, :].rearrange(
                    "(j p) f -> j p f", p=128).transpose([1, 0, 2])
                b.s_ld.inc(eng.dma_start(out=dst, in_=src), 16)
                b.sentinel(eng)
            self.on("sync", fsync_p)
            ldp = b.s_ld.n
            for tj in range(8):
                t = toff + tj

                def fsync(eng, t=t):
                    eng.wait_ge(b.s_v.h, b.s_v.n)
                    b.s_ld.inc(eng.dma_start(out=AFr[:, 0:1024],
                                             in_=b.xn_s[t * 128:(t + 1) * 128, :]), 16)
                    b.sentinel(eng)
                self.on("sync", fsync)
                ldv = b.s_ld.n

                def fvw(eng, ldv=ldv):
                    eng.wait_ge(b.s_ld.h, ldv)
                self.on("vector", fvw)
                self.on("vector", self.emit_quant(C, AFr[:, 0:1024], tcol=1024,
                                                  hi_ap=AQ[:, 0:1024],
                                                  lo_ap=AQ[:, 1024:2048]))

                def fv2(eng):
                    b.s_v.inc(eng.last)
                self.on("vector", fv2)
                vv = b.s_v.n

                pairs = []
                for ihl in range(2):
                    for kk in range(8):
                        pairs.append((AQ[:, ihl * 1024 + kk * 128:ihl * 1024 + (kk + 1) * 128],
                                      AT[:, ihl * 8192 + kk * 1024 + tj * 128:
                                           ihl * 8192 + kk * 1024 + (tj + 1) * 128]))
                self.emit_tr_blocks(pairs,
                                    pre_wait=lambda eng, vv=vv: eng.wait_ge(b.s_v.h, max(vv, b.s_v.n)))
            self.barrier()
            icol = 0
            n_tile = 0
            h_done = []
            pend = None
            GU = [(0, 1), (4, 5)]

            def emit_tail(pend, facc_only=False):
                (tj, IWp, nsubp, wd_bp, vq, ldw_p) = pend
                par = (n_tile - 1) % 2
                pairs = []
                for ihl in range(2):
                    for s in range(nsubp):
                        pairs.append((AQ[:, ihl * 512 + s * 128:ihl * 512 + (s + 1) * 128],
                                      AXNT[:, (ihl * nsubp + s) * 128:
                                           (ihl * nsubp + s + 1) * 128]))
                self.emit_tr_blocks(pairs,
                                    pre_wait=lambda eng, vq=vq: eng.wait_ge(b.s_v.h, vq))
                vve = b.s_v.n

                def hqT(ihl, s, nsubp=nsubp):
                    return AXNT[:, (ihl * nsubp + s) * 128:(ihl * nsubp + s + 1) * 128]

                def ften_d(eng, nsubp=nsubp, vve=vve, wd_bp=wd_bp, ldw_p=ldw_p):
                    eng.wait_ge(b.s_v.h, vve)
                    eng.wait_ge(b.s_ld.h, ldw_p)
                    last = None
                    for s in range(nsubp):
                        for pi, (hhl, whl) in enumerate(PASSES3):
                            for ch in range(2):
                                last = eng.matmul(out=b.ps[2 + ch][:, :],
                                           lhsT=hqT(hhl, s),
                                           rhs=wd_bp(s, whl)[:, ch * 512:(ch + 1) * 512],
                                           start=(s == 0 and pi == 0),
                                           stop=(s == nsubp - 1 and pi == 2))
                    b.s_t.inc(last)
                self.on("tensor", ften_d)
                tvd = b.s_t.n

                def facc(eng, tj=tj, tvd=tvd):
                    eng.wait_ge(b.s_t.h, tvd)
                    for ch in range(2):
                        a = AACC[:, tj * 1024 + ch * 512:tj * 1024 + (ch + 1) * 512]
                        eng.tensor_tensor(out=a, in0=a, in1=b.ps[2 + ch][:, :],
                                          op=OP.add)
                    b.s_v.inc(eng.last)
                self.on("vector", facc)
                return b.s_v.n

            prev_tv_gu = None
            for ic, IW in enumerate(I_CHUNKS):
                nsub = IW // 128
                if pend is not None:
                    emit_tail(pend)
                    pend = None

                def fsync1(eng, icol=icol, IW=IW,
                           gate=prev_tv_gu):
                    eng.wait_ge(b.s_t.h, gate if gate is not None else b.s_t.n)
                    for kh in range(2):
                        for wi, w_s in [(0, b.wg_s), (1, b.wu_s)]:
                            for a in range(2):
                                src_ = w_s[a, kh * 4:kh * 4 + 4, :,
                                           icol:icol + IW].transpose([1, 0, 2])
                                base = wi * 4096 + a * 8 * IW + kh * 4 * IW
                                dst = AW[:, base:base + 4 * IW].rearrange(
                                    "p (k q) -> p k q", k=4)
                                b.s_ld.inc(eng.dma_start(out=dst, in_=src_), 16)
                        if kh == 0:
                            ld_half[0] = b.s_ld.n
                ld_half = [0]
                self.on("sync", fsync1)
                ldv = b.s_ld.n
                ldv1 = ld_half[0]

                def fsync2(eng, icol=icol, nsub=nsub):
                    eng.wait_ge(b.s_t.h, b.s_t.n)
                    sub0 = icol // 128
                    src_ = b.wd_s[sub0:sub0 + nsub, :, :, :].transpose([2, 0, 1, 3])
                    dst = AW[:, 8192:8192 + nsub * 2048].rearrange(
                        "p (s a q) -> p s a q", s=nsub, a=2)
                    b.s_ld.inc(eng.dma_start(out=dst, in_=src_), 16)
                    b.sentinel(eng)
                self.on("sync", fsync2)
                ldw = b.s_ld.n

                def wgu_b(wi, ihl, kk, IW=IW):
                    off = wi * 4096 + (ihl * 8 + kk) * IW
                    return AW[:, off:off + IW]

                def wd_b(s, ihl):
                    off = 8192 + (s * 2 + ihl) * 1024
                    return AW[:, off:off + 1024]
                for tj in range(8):
                    par = n_tile % 2
                    g0, g1 = GU[par]
                    sil = par * 512
                    hcol = 1024 + par * 256

                    def ften_gu(eng, tj=tj, IW=IW, ldv=ldv, ldv1=ldv1, ldp=ldp,
                                g0=g0, g1=g1,
                                hfree=(h_done[n_tile - 2] if n_tile >= 2 else 0)):
                        eng.wait_ge(b.s_ld.h, max(ldv1, ldp))
                        eng.wait_ge(b.s_v.h, hfree)
                        last = None
                        for kk in range(8):
                            if kk == 4:
                                eng.wait_ge(b.s_ld.h, ldv)
                            for pi, (whl, xhl) in enumerate(PASSES3):
                                lhs = AT[:, xhl * 8192 + kk * 1024 + tj * 128:
                                         xhl * 8192 + kk * 1024 + (tj + 1) * 128]
                                eng.matmul(out=b.ps[g0][:, 0:IW], lhsT=lhs,
                                           rhs=wgu_b(0, whl, kk),
                                           start=(kk == 0 and pi == 0),
                                           stop=(kk == 7 and pi == 2))
                                last = eng.matmul(out=b.ps[g1][:, 0:IW], lhsT=lhs,
                                                  rhs=wgu_b(1, whl, kk),
                                                  start=(kk == 0 and pi == 0),
                                                  stop=(kk == 7 and pi == 2))
                        b.s_t.inc(last)
                    self.on("tensor", ften_gu)
                    tv = b.s_t.n

                    def fscl(eng, IW=IW, tv=tv, g0=g0, sil=sil):
                        eng.wait_ge(b.s_t.h, tv)
                        eng.activation(out=AFr[:, sil:sil + IW], in_=b.ps[g0][:, 0:IW],
                                       func=AF.Silu, bias=np.float32(0.0),
                                       scale=np.float32(1.0))
                        b.s_s.inc(eng.last)
                    self.on("scalar", fscl)
                    sv = b.s_s.n

                    if pend is not None:
                        emit_tail(pend)
                        pend = None

                    def fvec_h(eng, IW=IW, sv=sv, g1=g1, sil=sil, hcol=hcol):
                        eng.wait_ge(b.s_s.h, sv)
                        eng.tensor_tensor(out=AFr[:, hcol:hcol + IW],
                                          in0=AFr[:, sil:sil + IW],
                                          in1=b.ps[g1][:, 0:IW], op=OP.mult)
                    self.on("vector", fvec_h)
                    self.on("vector", self.emit_quant(IW, AFr[:, hcol:hcol + IW],
                                                      tcol=2048,
                                                      hi_ap=AQ[:, 0:IW],
                                                      lo_ap=AQ[:, 512:512 + IW]))

                    def fv2(eng):
                        b.s_v.inc(eng.last)
                    self.on("vector", fv2)
                    h_done.append(b.s_v.n)
                    n_tile += 1
                    pend = (tj, IW, nsub, wd_b, b.s_v.n, ldw)
                    if (not PIPE_B) or ic == len(I_CHUNKS) - 1:
                        vvf = emit_tail(pend)
                        pend = None
                    if ic == len(I_CHUNKS) - 1:

                        def fout(eng, bi=bi, toff=toff, tj=tj, vvf=vvf):
                            eng.wait_ge(b.s_v.h, vvf)
                            b.s_st.inc(eng.dma_start(
                                out=b.out[bi, (toff + tj) * 128:(toff + tj + 1) * 128, :],
                                in_=AACC[:, tj * 1024:(tj + 1) * 1024]), 16)
                        self.on("gpsimd", fout)
                prev_tv_gu = tv
                icol += IW
            self.barrier()
        self.psum_close(2)

    def emit_scratch_dump(self):
        b = self

        def fd(eng):
            eng.wait_ge(b.s_v.h, b.s_v.n)
            eng.wait_ge(b.s_s.h, b.s_s.n)
            eng.wait_ge(b.s_t.h, b.s_t.n)
            eng.wait_ge(b.s_st.h, b.s_st.n)
            b.s_st.inc(eng.dma_start(out=b.dbg2[0], in_=b.xn_s[:, :]), 16)
            b.s_st.inc(eng.dma_start(out=b.dbg2[1], in_=b.part_s[:, :]), 16)
            b.s_st.inc(eng.dma_start(out=b.dbg2[2], in_=b.v_s[:, :]), 16)
            b.s_st.inc(eng.dma_start(out=b.dbg2[3, 0:128, :],
                                     in_=b.kt_s[0, 0, :, :].bitcast(F32)), 16)
        self.on("gpsimd", fd)

    def emit_debug_dump(self):
        b = self
        def fd(eng):
            eng.wait_ge(b.s_v.h, b.s_v.n)
            eng.wait_ge(b.s_s.h, b.s_s.n)
            eng.wait_ge(b.s_t.h, b.s_t.n)
            b.s_st.inc(eng.dma_start(out=b.dbg[0, :, 0:8192],
                                     in_=b.AT.ap().bitcast(F32)), 16)
            b.s_st.inc(eng.dma_start(out=b.dbg[1, :, 0:4096],
                                     in_=b.AXNT.ap().bitcast(F32)), 16)
            b.s_st.inc(eng.dma_start(out=b.dbg[2, :, 0:8192], in_=b.AFr[:, 0:8192]), 16)
        self.on("gpsimd", fd)

    def finish(self):
        b = self
        stv = b.s_st.n

        def fdone(eng, stv=stv):
            eng.wait_ge(b.s_st.h, stv)
        self.on("gpsimd", fdone)
        self.stack.close()
        return self.nc


def build_nc(stage="full"):
    b = Builder()
    b.emit_setup()
    if stage != "setup":
        b.emit_weight_prep()
        b.emit_bias_setup()
        b.barrier()
    if stage in ("full", "a1", "a2", "a3", "b1", "a_all", "ab1", "f2a"):
        nb = 2 if stage in ("full", "f2a") else 1
        nch = NCH if stage in ("full", "a_all", "ab1", "f2a") else 1
        for bi in range(nb):
            if stage != "b1":
                for c in range(nch):
                    b._a1(bi, c)
                    b.barrier()
                    if stage == "a1":
                        b.emit_debug_dump()
                        break
                    b._a2(bi, c)
                    if stage == "a2":
                        b.barrier()
                        b.emit_debug_dump()
                        break
                    b._a3(bi, c)
                    if c == nch - 1 or stage == "a3":
                        b.barrier()
            if stage in ("a_all",):
                b.emit_scratch_dump()
            if stage in ("full", "b1", "ab1") or (stage == "f2a" and bi == 0):
                b.emit_phase_b(bi)
                b.barrier()
                if stage == "ab1":
                    b.emit_scratch_dump()
    if stage == "full" and DEBUG:
        b.emit_scratch_dump()
    return b.finish()


def kernel(L0_attn_x, L0_attn_W_qkv, L0_attn_b_qkv, L0_attn_W_proj, L0_attn_b_proj,
           L0_ffn_w_gate, L0_ffn_w_up, L0_ffn_w_down):
    x = np.ascontiguousarray(np.asarray(L0_attn_x, dtype=np.float32))
    tabs = _host_tables()
    btabs = _host_bias_tables(np.asarray(L0_attn_b_qkv, dtype=np.float32),
                              np.asarray(L0_attn_b_proj, dtype=np.float32))
    nc = build_nc()
    base = {
        "wqkv": np.ascontiguousarray(np.asarray(L0_attn_W_qkv, dtype=np.float32)),
        "wproj": np.ascontiguousarray(np.asarray(L0_attn_W_proj, dtype=np.float32)),
        "wg_in": np.ascontiguousarray(np.asarray(L0_ffn_w_gate, dtype=np.float32)),
        "wu_in": np.ascontiguousarray(np.asarray(L0_ffn_w_up, dtype=np.float32)),
        "wd_in": np.ascontiguousarray(np.asarray(L0_ffn_w_down, dtype=np.float32)),
        **tabs, **btabs,
    }
    in_maps = []
    for core in range(8):
        m = dict(base)
        m["x"] = np.ascontiguousarray(x[2 * core:2 * core + 2])
        in_maps.append(m)
    res = run_bass_kernel_spmd(nc, in_maps, core_ids=list(range(8)))
    out = np.concatenate([np.asarray(r["out"]) for r in res.results], axis=0)
    return out.astype(np.float32)


# version 10
# speedup vs baseline: 1.0077x; 1.0077x over previous
import contextlib
import numpy as np
import ml_dtypes
import concourse.bass as bass
import concourse.mybir as mybir
from concourse.bass_utils import run_bass_kernel_spmd
from concourse.alu_op_type import AluOpType as OP

F32 = mybir.dt.float32
BF16 = mybir.dt.bfloat16
AF = mybir.ActivationFunctionType
BF = ml_dtypes.bfloat16
f32 = np.float32

B_PER_CORE = 2
T = 2048
C = 1024
NG = 8
I_FF = 2816
NT = T // 128
NCH = T // 512
I_CHUNKS = [256] * 11
PASSES3 = [(0, 0), (0, 1), (1, 0)]
SWAP_MASK = [i ^ 1 for i in range(32)]
A1_WAVES = [("q", 0), ("k", C), ("v", 2 * C)]
PREP_PARTS = ("qkv", "proj", "wg", "wu", "wd")
WG_MODE = "full"
DEBUG = False
PIPE_A1 = True
PIPE_A2 = True
PIPE_A3 = True
PIPE_B = True

AFR_W = 8704
BV_COL = 6656
BP_COL = 7680
TC_COS = 0
TC_SIN = 2048
TC_BQ = 4096
TC_BK = 4104
TC_ST = 4112
TC_DEN = 4120
TAB_W = 4640


def _host_tables():
    import jax
    import jax.numpy as jnp
    t = {}
    with jax.default_device(jax.devices("cpu")[0]):
        pos = jnp.arange(T, dtype=jnp.float32)[:, None]
        i = jnp.arange(1, 33, dtype=jnp.float32)[None]
        theta = 1.0 / 10000.0 ** (2.0 * (i - 1.0) / 64.0)
        ang = pos * theta
        cosv = np.asarray(jnp.cos(ang), dtype=np.float64)
        sinv = np.asarray(jnp.sin(ang), dtype=np.float64)
    p = np.arange(128)
    jrow = (p % 64) // 2
    sgn = np.where(p % 2 == 0, -1.0, 1.0)
    cosT = cosv[:, jrow].T
    sinT = sinv[:, jrow].T * sgn[:, None]
    t["cos_h"] = cosT.astype(f32)
    t["cos_l"] = np.zeros_like(t["cos_h"])
    t["sin_h"] = sinT.astype(f32)
    t["sin_l"] = np.zeros_like(t["sin_h"])
    m = np.zeros((4, 128, 512), dtype=f32)
    for jj in range(4):
        kt = 128 * jj + np.arange(128)[:, None]
        q = np.arange(512)[None, :]
        m[jj] = np.where(kt > q, -1e9, 0.0)
    t["masks"] = m.astype(BF)
    t["ident"] = np.eye(128, dtype=BF)
    return t


def _host_bias_tables(b_qkv, b_proj):
    t = {}
    p = np.arange(128)
    bq = np.zeros((128, NG), dtype=f32)
    bk = np.zeros((128, NG), dtype=f32)
    for g in range(NG):
        head = 2 * g + p // 64
        d = p % 64
        bq[:, g] = b_qkv[192 * head + d]
        bk[:, g] = b_qkv[192 * head + 64 + d]
    t["bq_tab"] = bq
    t["bk_tab"] = bk
    jcol = np.arange(C)
    bv = b_qkv[192 * (jcol // 64) + 128 + (jcol % 64)]
    t["bv_bc"] = np.broadcast_to(bv.astype(f32), (128, C)).copy()
    t["bp_bc"] = np.broadcast_to(b_proj.astype(f32), (128, C)).copy()
    return t


class _EngP:
    _NOTRACK = {"wait_ge", "drain", "nop", "engine_nop", "sem_inc"}

    def __init__(self, eng, store, key):
        object.__setattr__(self, "_e", eng)
        object.__setattr__(self, "_store", store)
        object.__setattr__(self, "_key", key)

    @property
    def last(self):
        return self._store[self._key]

    def __getattr__(self, n):
        f = getattr(self._e, n)
        if callable(f) and not n.startswith("_"):
            track = n not in _EngP._NOTRACK
            store, key = self._store, self._key

            def w(*a, **k):
                r = f(*a, **k)
                if track and hasattr(r, "then_inc"):
                    store[key] = r
                return r
            return w
        return f


class Sem:
    def __init__(self, stack, nc, name):
        self.h = stack.enter_context(nc.semaphore(name))
        self.n = 0

    def inc(self, instr, k=1):
        instr.then_inc(self.h, k)
        self.n += k
        return self.n


class Builder:
    def __init__(self):
        self.nc = bass.Bass("TRN2", target_bir_lowering=False,
                            detect_race_conditions=False)
        nc = self.nc

        def dram(n, s, d, k=None):
            h = nc.dram_tensor(n, s, d, kind=k) if k else nc.dram_tensor(n, s, d)
            return h.ap()
        self.x = dram("x", [B_PER_CORE, T, C], F32, "ExternalInput")
        self.wqkv = dram("wqkv", [C, 3 * C], F32, "ExternalInput")
        self.wproj = dram("wproj", [C, C], F32, "ExternalInput")
        self.wg_in = dram("wg_in", [I_FF, C], F32, "ExternalInput")
        self.wu_in = dram("wu_in", [I_FF, C], F32, "ExternalInput")
        self.wd_in = dram("wd_in", [C, I_FF], F32, "ExternalInput")
        for n in ["cos_h", "cos_l", "sin_h", "sin_l"]:
            setattr(self, n, dram(n, [128, T], F32, "ExternalInput"))
        self.masks = dram("masks", [4, 128, 512], BF16, "ExternalInput")
        self.ident = dram("ident", [128, 128], BF16, "ExternalInput")
        self.bq_tab = dram("bq_tab", [128, NG], F32, "ExternalInput")
        self.bk_tab = dram("bk_tab", [128, NG], F32, "ExternalInput")
        self.bv_bc = dram("bv_bc", [128, C], F32, "ExternalInput")
        self.bp_bc = dram("bp_bc", [128, C], F32, "ExternalInput")
        self.out = dram("out", [B_PER_CORE, T, C], F32, "ExternalOutput")
        self.wqkv_s = dram("wqkv_s", [8, 2, 128, 3 * C], BF16)
        self.wproj_s = dram("wproj_s", [8, 2, 128, C], BF16)
        self.wg_s = dram("wg_s", [2, 8, 128, I_FF], BF16)
        self.wu_s = dram("wu_s", [2, 8, 128, I_FF], BF16)
        self.wd_s = dram("wd_s", [22, 2, 128, C], BF16)
        self.xn_s = dram("xn_s", [T, C], F32)
        self.kt_s = dram("kt_s", [NG, 2, 128, T], BF16)
        self.v_s = dram("v_s", [T, C], F32)
        self.part_s = dram("part_s", [T, C], F32)
        if DEBUG:
            self.dbg = dram("dbg", [3, 128, 16384], F32)
            self.dbg2 = dram("dbg2", [4, T, C], F32, "ExternalOutput")

        self.stack = contextlib.ExitStack()
        ec = self.stack.enter_context

        def sb(n, cols, dt):
            return ec(nc.sbuf_tensor(n, [128, cols], dt))
        self.AW = sb("AW", 12544, BF16)
        self.AT = sb("AT", 16384, BF16)
        self.AXNT = sb("AXNT", 8192, BF16)
        self.AACC = sb("AACC", 8192, F32)
        self.AFr = sb("AF", AFR_W, F32)
        self.AQ = sb("AQ", 5664, BF16)
        self.TAB = sb("TAB", TAB_W, F32)
        self.MSK = sb("MSK", 2048, BF16)
        self.KBUF = sb("KBUF", 1024, BF16)
        self.VBUF = sb("VBUF", 512, F32)
        self.ID16 = sb("ID16", 128, BF16)
        self.ID32 = sb("ID32", 128, F32)
        self.ONES16 = sb("ONES16", 16, BF16)
        self.ONESF = sb("ONESF", 64, F32)
        self.ps = [ec(nc.psum_tensor(f"pb{i}", [128, 512], F32)) for i in range(6)]
        self._psum_scoped = []

        self.s_bar = Sem(self.stack, nc, "s_bar")
        self.s_ld = Sem(self.stack, nc, "s_ld")
        self.s_st = Sem(self.stack, nc, "s_st")
        self.s_t = Sem(self.stack, nc, "s_t")
        self.s_v = Sem(self.stack, nc, "s_v")
        self.s_s = Sem(self.stack, nc, "s_s")
        self.block = ec(nc.Block())

    def on(self, eng_name, fn):
        if not hasattr(self, "_lasts"):
            self._lasts = {}
        self._lasts.setdefault(eng_name, None)

        def wrapped(eng):
            fn(_EngP(eng, self._lasts, eng_name))
        getattr(self.block, eng_name)(wrapped)

    def psum_open(self, name, cols, dt):
        self._psn = getattr(self, "_psn", 0) + 1
        g = self.nc.psum_tensor(f"{name}{self._psn}", [128, cols], dt)
        t = g.__enter__()
        self._psum_scoped.append(g)
        return t

    def psum_close(self, n=1):
        for _ in range(n):
            g = self._psum_scoped.pop()
            g.__exit__(None, None, None)

    def sentinel(self, eng):
        self.s_ld.inc(eng.dma_start(out=self.TAB[:, 4632:4640], in_=self.bq_tab), 16)

    def emit_tr_blocks(self, pairs, pre_wait=None):
        b = self
        t0 = b.s_t.n
        v0 = b.s_v.n
        banks = [self.trA, self.trB]

        def ften(eng):
            if pre_wait is not None:
                pre_wait(eng)
            for i, (s, d) in enumerate(pairs):
                if i >= 2:
                    eng.wait_ge(b.s_v.h, v0 + i - 1)
                b.s_t.inc(eng.transpose(out=banks[i % 2][:, :], in_=s,
                                        identity=b.ID16[:, :]))
        self.on("tensor", ften)

        def fvec(eng):
            for i, (s, d) in enumerate(pairs):
                eng.wait_ge(b.s_t.h, t0 + i + 1)
                b.s_v.inc(eng.tensor_copy(out=d, in_=banks[i % 2][:, :]))
        self.on("vector", fvec)

    def barrier(self):
        vals = [(self.s_t, self.s_t.n), (self.s_v, self.s_v.n),
                (self.s_s, self.s_s.n), (self.s_st, self.s_st.n),
                (self.s_ld, self.s_ld.n)]
        for e in ["sync", "gpsimd", "tensor", "vector", "scalar"]:
            def fn(eng, vals=vals):
                for s, n in vals:
                    eng.wait_ge(s.h, n)
            self.on(e, fn)

    def emit_setup(self):
        b = self
        TAB, AFr = b.TAB, b.AFr

        def fsync(eng):
            for src, dst in [(b.cos_h, AFr[:, 0:2048]), (b.cos_l, AFr[:, 2048:4096]),
                             (b.sin_h, AFr[:, 4096:6144]), (b.sin_l, AFr[:, 6144:8192])]:
                b.s_ld.inc(eng.dma_start(out=dst, in_=src), 16)
            b.s_ld.inc(eng.dma_start(out=TAB[:, TC_BQ:TC_BQ + NG], in_=b.bq_tab), 16)
            b.s_ld.inc(eng.dma_start(out=TAB[:, TC_BK:TC_BK + NG], in_=b.bk_tab), 16)
            b.s_ld.inc(eng.dma_start(
                out=b.MSK[:, 0:2048].rearrange("p (m q) -> p m q", m=4),
                in_=b.masks.transpose([1, 0, 2])), 16)
            b.s_ld.inc(eng.dma_start(out=b.ID16[:, :], in_=b.ident), 16)
        self.on("sync", fsync)
        ldv = b.s_ld.n

        def fvec(eng, ldv=ldv):
            eng.wait_ge(b.s_ld.h, ldv)
            eng.tensor_tensor(out=TAB[:, TC_COS:TC_COS + 2048],
                              in0=AFr[:, 0:2048], in1=AFr[:, 2048:4096], op=OP.add)
            eng.tensor_tensor(out=TAB[:, TC_SIN:TC_SIN + 2048],
                              in0=AFr[:, 4096:6144], in1=AFr[:, 6144:8192], op=OP.add)
            eng.memset(b.ONES16[:, :], 1.0)
            eng.memset(b.ONESF[:, :], 1.0)
            eng.tensor_copy(out=b.ID32[:, :], in_=b.ID16[:, :])
            b.s_v.inc(eng.last)
        self.on("vector", fvec)

    def emit_quant(self, F, x_ap, tcol, scol=5640, hi_ap=None, lo_ap=None):
        b = self
        A = b.AFr
        nb = F // 16
        S = A[:, scol:scol + nb]
        R = A[:, scol + nb:scol + 2 * nb]
        t0 = A[:, tcol:tcol + F]
        t1 = A[:, tcol + F:tcol + 2 * F]
        t2 = A[:, tcol + 2 * F:tcol + 3 * F]
        t3 = A[:, tcol + 3 * F:tcol + 4 * F]

        def fn(eng):
            eng.tensor_reduce(out=S.rearrange("p (n o) -> p n o", o=1),
                              in_=x_ap.rearrange("p (n s) -> p n s", s=16),
                              axis=mybir.AxisListType.X, op=OP.max,
                              apply_absolute_value=True)
            eng.drain()
            eng.tensor_scalar(out=S, in0=S, scalar1=1.0 / 6.0, scalar2=1e-12,
                              op0=OP.mult, op1=OP.max)
            eng.drain()
            eng.reciprocal(out=R, in_=S)
            eng.drain()
            Rb = R.rearrange("p (n o) -> p n o", o=1).broadcast_to((128, nb, 16))
            Sb = S.rearrange("p (n o) -> p n o", o=1).broadcast_to((128, nb, 16))
            x3 = x_ap.rearrange("p (n s) -> p n s", s=16)
            eng.tensor_tensor(out=t0.rearrange("p (n s) -> p n s", s=16),
                              in0=x3, in1=Rb, op=OP.mult)
            eng.scalar_tensor_tensor(out=t1, in0=t0, scalar=-1.0, in1=t0,
                                     op0=OP.mult, op1=OP.max)
            eng.tensor_scalar(out=t2, in0=t1, scalar1=0.25, scalar2=None, op0=OP.is_gt)
            for thr in (0.75, 1.25, 1.75):
                eng.scalar_tensor_tensor(out=t2, in0=t1, scalar=thr, in1=t2,
                                         op0=OP.is_gt, op1=OP.add)
            eng.tensor_scalar(out=t3, in0=t1, scalar1=2.5, scalar2=None, op0=OP.is_gt)
            eng.scalar_tensor_tensor(out=t3, in0=t1, scalar=3.5, in1=t3,
                                     op0=OP.is_gt, op1=OP.add)
            eng.tensor_scalar(out=t1, in0=t1, scalar1=5.0, scalar2=None, op0=OP.is_gt)
            eng.scalar_tensor_tensor(out=t3, in0=t1, scalar=2.0, in1=t3,
                                     op0=OP.mult, op1=OP.add)
            eng.scalar_tensor_tensor(out=t2, in0=t2, scalar=0.5, in1=t3,
                                     op0=OP.mult, op1=OP.add)
            eng.tensor_scalar(out=t1, in0=t0, scalar1=0.0, scalar2=None, op0=OP.is_lt)
            eng.tensor_tensor(out=t1, in0=t2, in1=t1, op=OP.mult)
            eng.scalar_tensor_tensor(out=t2, in0=t1, scalar=-2.0, in1=t2,
                                     op0=OP.mult, op1=OP.add)
            eng.tensor_tensor(out=x_ap.rearrange("p (n s) -> p n s", s=16),
                              in0=t2.rearrange("p (n s) -> p n s", s=16),
                              in1=Sb, op=OP.mult)
            if hi_ap is not None:
                eng.tensor_copy(out=hi_ap, in_=x_ap)
                eng.tensor_tensor(out=lo_ap, in0=x_ap, in1=hi_ap, op=OP.subtract)
        return fn

    def emit_weight_prep(self):
        b = self
        AFr, AW, AQ = b.AFr, b.AW, b.AQ
        for name, w_in, w_s, width in [("qkv", b.wqkv, b.wqkv_s, 3 * C),
                                       ("proj", b.wproj, b.wproj_s, C)]:
            if name not in PREP_PARTS:
                continue
            sec_v0, sec_st0 = b.s_v.n, b.s_st.n
            fv_n, st_n = [], []
            for kk in range(8):
                par = kk % 2
                src = AFr[:, par * 3072:par * 3072 + width]
                hi = AW[:, par * 3072:par * 3072 + width]
                lo = AW[:, 6272 + par * 3072:6272 + par * 3072 + width]

                def fsync(eng, kk=kk, w_in=w_in, src=src,
                          wv=(fv_n[kk - 2] if kk >= 2 else sec_v0)):
                    eng.wait_ge(b.s_v.h, wv)
                    b.s_ld.inc(eng.dma_start(out=src,
                                             in_=w_in[kk * 128:(kk + 1) * 128, :]), 16)
                    b.sentinel(eng)
                self.on("sync", fsync)
                ldv = b.s_ld.n

                def fvec(eng, ldv=ldv, src=src, hi=hi, lo=lo,
                         ws=(st_n[kk - 2] if kk >= 2 else sec_st0)):
                    eng.wait_ge(b.s_ld.h, ldv)
                    eng.wait_ge(b.s_st.h, ws)
                    eng.tensor_copy(out=hi, in_=src)
                    eng.tensor_tensor(out=lo, in0=src, in1=hi, op=OP.subtract)
                    b.s_v.inc(eng.last)
                self.on("vector", fvec)
                fv_n.append(b.s_v.n)
                vv = b.s_v.n

                def fst(eng, kk=kk, name=name, w_s=w_s, width=width, vv=vv,
                        hi=hi, lo=lo):
                    eng.wait_ge(b.s_v.h, vv)
                    for ihl, buf in [(0, hi), (1, lo)]:
                        if name == "qkv":
                            for sect in range(3):
                                src_ = buf.rearrange(
                                    "p (h s d) -> p h s d", h=16, s=3)[:, :, sect, :]
                                dst = w_s[kk, ihl, :, sect * C:(sect + 1) * C].rearrange(
                                    "p (h d) -> p h d", h=16)
                                b.s_st.inc(eng.dma_start(out=dst, in_=src_), 16)
                        else:
                            b.s_st.inc(eng.dma_start(out=w_s[kk, ihl, :, :],
                                                     in_=buf), 16)
                self.on("gpsimd", fst)
                st_n.append(b.s_st.n)
        self.trA = self.psum_open("trA", 128, BF16)
        self.trB = self.psum_open("trB", 128, BF16)
        for wname, w_in, w_s in [("wg", b.wg_in, b.wg_s), ("wu", b.wu_in, b.wu_s)]:
            if wname not in PREP_PARTS:
                continue
            sec_v0, sec_st0, sec_t0 = b.s_v.n, b.s_st.n, b.s_t.n
            qv_n, tv_n, st_n = [], [], []
            for it in range(22):
                par = it % 2
                src = AFr[:, 5824:5824 + C] if par == 0 else AFr[:, 4608:4608 + C]
                hl = AQ if par == 0 else AW
                hi, lo = hl[:, 0:C], hl[:, 1024:2048]
                trn = hl[:, 2048:4096]

                def fsync(eng, w_in=w_in, it=it, src=src,
                          wv=(qv_n[it - 2] if it >= 2 else sec_v0)):
                    eng.wait_ge(b.s_v.h, wv)
                    b.s_ld.inc(eng.dma_start(out=src,
                                             in_=w_in[it * 128:(it + 1) * 128, :]), 16)
                    b.sentinel(eng)
                self.on("sync", fsync)
                ldv = b.s_ld.n

                def fvw(eng, ldv=ldv, it=it,
                        wt=(tv_n[it - 2] if it >= 2 else sec_t0)):
                    eng.wait_ge(b.s_ld.h, ldv)
                    eng.wait_ge(b.s_t.h, wt)
                    if it < 2:
                        eng.wait_ge(b.s_st.h, sec_st0)
                self.on("vector", fvw)
                self.on("vector", self.emit_quant(C, src, tcol=0,
                                                  hi_ap=hi, lo_ap=lo))

                def fv2(eng):
                    b.s_v.inc(eng.last)
                self.on("vector", fv2)
                qv_n.append(b.s_v.n)
                vv = b.s_v.n

                pairs = []
                for ihl in range(2):
                    for kk in range(8):
                        pairs.append((hl[:, ihl * 1024 + kk * 128:ihl * 1024 + (kk + 1) * 128],
                                      trn[:, kk * 128 + ihl * 1024:
                                           kk * 128 + ihl * 1024 + 128]))
                ws = st_n[it - 2] if it >= 2 else sec_st0
                self.emit_tr_blocks(
                    pairs, pre_wait=lambda eng, vv=vv, ws=ws: (
                        eng.wait_ge(b.s_v.h, vv), eng.wait_ge(b.s_st.h, ws)))
                tv_n.append(b.s_t.n)
                vv2 = b.s_v.n

                def fst(eng, it=it, w_s=w_s, vv2=vv2, trn=trn):
                    eng.wait_ge(b.s_v.h, vv2)
                    for ihl in range(2):
                        dst = w_s[ihl, :, :, it * 128:(it + 1) * 128].transpose([1, 0, 2])
                        src_ = trn[:, ihl * 1024:(ihl + 1) * 1024].rearrange(
                            "p (k q) -> p k q", k=8)
                        b.s_st.inc(eng.dma_start(out=dst, in_=src_), 16)
                self.on("gpsimd", fst)
                st_n.append(b.s_st.n)
        sec_v0, sec_st0, sec_t0 = b.s_v.n, b.s_st.n, b.s_t.n
        qv_n, tv_n, st_n = [], [], []
        for ht in (range(8) if "wd" in PREP_PARTS else []):
            par = ht % 2
            sA, sc = (AFr, 5824) if par == 0 else (b.AACC, 0)
            src = sA[:, sc:sc + I_FF]

            def fsync(eng, ht=ht, src=src,
                      wv=(qv_n[ht - 2] if ht >= 2 else sec_v0)):
                eng.wait_ge(b.s_v.h, wv)
                b.s_ld.inc(eng.dma_start(out=src,
                                         in_=b.wd_in[ht * 128:(ht + 1) * 128, :]), 16)
                b.sentinel(eng)
            self.on("sync", fsync)
            ldv = b.s_ld.n

            def fvw(eng, ldv=ldv, ht=ht,
                    wt=(tv_n[ht - 1] if ht >= 1 else sec_t0)):
                eng.wait_ge(b.s_ld.h, ldv)
                eng.wait_ge(b.s_t.h, wt)
                if ht < 1:
                    eng.wait_ge(b.s_st.h, sec_st0)
            self.on("vector", fvw)
            for half in range(2):
                self.on("vector", self.emit_quant(
                    1408, sA[:, sc + half * 1408:sc + half * 1408 + 1408], tcol=0,
                    hi_ap=AQ[:, half * 1408:half * 1408 + 1408],
                    lo_ap=AQ[:, 2816 + half * 1408:2816 + half * 1408 + 1408]))

            def fv2(eng):
                b.s_v.inc(eng.last)
            self.on("vector", fv2)
            qv_n.append(b.s_v.n)
            vvq = b.s_v.n
            pairs = []
            for ihl in range(2):
                for s in range(22):
                    pairs.append((AQ[:, ihl * 2816 + s * 128:ihl * 2816 + (s + 1) * 128],
                                  b.AW[:, ihl * 2816 + s * 128:ihl * 2816 + (s + 1) * 128]))
            ws = st_n[ht - 1] if ht >= 1 else sec_st0
            self.emit_tr_blocks(
                pairs, pre_wait=lambda eng, vvq=vvq, ws=ws: (
                    eng.wait_ge(b.s_v.h, vvq), eng.wait_ge(b.s_st.h, ws)))
            tv_n.append(b.s_t.n)
            vv2 = b.s_v.n

            def fst(eng, ht=ht, vv2=vv2):
                eng.wait_ge(b.s_v.h, vv2)
                for ihl in range(2):
                    dst = b.wd_s[:, ihl, :, ht * 128:(ht + 1) * 128].transpose([1, 0, 2])
                    src = b.AW[:, ihl * 2816:ihl * 2816 + 2816].rearrange(
                        "p (k q) -> p k q", k=22)
                    b.s_st.inc(eng.dma_start(out=dst, in_=src), 16)
            self.on("gpsimd", fst)
            st_n.append(b.s_st.n)
        self.psum_close(2)
        self.barrier()

    def emit_bias_setup(self):
        b = self

        def fsync(eng):
            b.s_ld.inc(eng.dma_start(out=b.AFr[:, BV_COL:BV_COL + C], in_=b.bv_bc), 16)
            b.s_ld.inc(eng.dma_start(out=b.AFr[:, BP_COL:BP_COL + C], in_=b.bp_bc), 16)
        self.on("sync", fsync)

    def _a1(self, bi, c):
        b = self
        AFr, AT, AXNT, AACC, AW, TAB, AQ = b.AFr, b.AT, b.AXNT, b.AACC, b.AW, b.TAB, b.AQ
        self.trA = self.psum_open("trA", 128, BF16)
        self.trB = self.psum_open("trB", 128, BF16)
        if not hasattr(self, "_ln_q"):
            self._ln_q = []
            self._ln_st = []
        for j in range(4):
            t = 4 * c + j
            p = j % 2
            gi = len(self._ln_q)
            xb = AFr[:, 0:1024] if p == 0 else AFr[:, 4096:5120]
            xn = AFr[:, 1024:2048] if p == 0 else AFr[:, 5120:6144]
            aqb = 0 if p == 0 else 2048
            STB = TC_ST + p * 8

            def fsync(eng, t=t, xb=xb,
                      wq=(self._ln_q[gi - 2] if gi >= 2 else None)):
                eng.wait_ge(b.s_v.h, wq if wq is not None else b.s_v.n)
                b.s_ld.inc(eng.dma_start(out=xb,
                                         in_=b.x[bi, t * 128:(t + 1) * 128, :]), 16)
                b.sentinel(eng)
            self.on("sync", fsync)
            ldv = b.s_ld.n

            def fscl(eng, ldv=ldv, xb=xb, xn=xn, STB=STB,
                     ws=(self._ln_st[gi - 2] if gi >= 2 else None)):
                eng.wait_ge(b.s_ld.h, ldv)
                eng.wait_ge(b.s_st.h, ws if ws is not None else b.s_st.n)
                st = TAB[:, STB:STB + 1]
                st2 = TAB[:, STB + 1:STB + 2]
                eng.activation(out=xn, in_=xb, func=AF.Copy, accum_out=st)
                eng.activation(out=xn, in_=xb, func=AF.Square, accum_out=st2)
                b.s_s.inc(eng.last)
            self.on("scalar", fscl)
            sv = b.s_s.n

            def fvec(eng, sv=sv, STB=STB):
                eng.wait_ge(b.s_s.h, sv)
                st = TAB[:, STB:STB + 1]
                st2 = TAB[:, STB + 1:STB + 2]
                mu = TAB[:, STB + 2:STB + 3]
                var = TAB[:, STB + 3:STB + 4]
                eng.tensor_scalar(out=mu, in0=st, scalar1=1.0 / C, scalar2=None, op0=OP.mult)
                eng.tensor_scalar(out=var, in0=st2, scalar1=1.0 / C, scalar2=None, op0=OP.mult)
                eng.drain()
                eng.tensor_tensor(out=st, in0=mu, in1=mu, op=OP.mult)
                eng.drain()
                eng.tensor_tensor(out=var, in0=var, in1=st, op=OP.subtract)
                b.s_v.inc(eng.last)
            self.on("vector", fvec)
            vv = b.s_v.n

            def fscl2(eng, vv=vv, STB=STB):
                eng.wait_ge(b.s_v.h, vv)
                eng.activation(out=TAB[:, STB + 4:STB + 5],
                               in_=TAB[:, STB + 3:STB + 4], func=AF.Sqrt,
                               bias=np.float32(1e-5), scale=np.float32(1.0))
                b.s_s.inc(eng.last)
            self.on("scalar", fscl2)
            sv2 = b.s_s.n

            def fvec2(eng, sv2=sv2, xb=xb, xn=xn, aqb=aqb, STB=STB):
                eng.wait_ge(b.s_s.h, sv2)
                rs = TAB[:, STB + 5:STB + 6]
                mu = TAB[:, STB + 2:STB + 3]
                eng.reciprocal(out=rs, in_=TAB[:, STB + 4:STB + 5])
                eng.drain()
                eng.tensor_scalar(out=xn, in0=xb,
                                  scalar1=mu[:, 0:1], scalar2=rs[:, 0:1],
                                  op0=OP.subtract, op1=OP.mult)
                eng.tensor_copy(out=AQ[:, aqb:aqb + 1024], in_=xn)
                eng.tensor_tensor(out=AQ[:, aqb + 1024:aqb + 2048], in0=xn,
                                  in1=AQ[:, aqb:aqb + 1024], op=OP.subtract)
                b.s_v.inc(eng.last)
            self.on("vector", fvec2)
            vv2 = b.s_v.n
            self._ln_q.append(vv2)

            def fst(eng, t=t, vv2=vv2, xn=xn):
                eng.wait_ge(b.s_v.h, vv2)
                b.s_st.inc(eng.dma_start(out=b.xn_s[t * 128:(t + 1) * 128, :],
                                         in_=xn), 16)
            self.on("gpsimd", fst)
            self._ln_st.append(b.s_st.n)

            pairs = []
            for ihl in range(2):
                for kk in range(8):
                    pairs.append((AQ[:, aqb + ihl * 1024 + kk * 128:
                                     aqb + ihl * 1024 + (kk + 1) * 128],
                                  AXNT[:, ihl * 4096 + kk * 512 + j * 128:
                                       ihl * 4096 + kk * 512 + (j + 1) * 128]))
            self.emit_tr_blocks(
                pairs, pre_wait=lambda eng, vv2=vv2: eng.wait_ge(b.s_v.h, max(vv2, b.s_v.n)))
        vvx = b.s_v.n
        self.psum_close(2)
        pse = [self.psum_open("pe6", 512, F32), self.psum_open("pe7", 512, F32)]
        bank8 = b.ps + pse

        def xnT(ihl, kk):
            return b.AXNT[:, ihl * 4096 + kk * 512:ihl * 4096 + (kk + 1) * 512]
        for wave, colbase in A1_WAVES:
            tv_kk = []
            wave_t0 = b.s_t.n
            for kk in range(8):
                def fsync(eng, kk=kk, colbase=colbase,
                          tneed=None):
                    slot = kk % 2
                    depth = 2 if PIPE_A1 else 1
                    eng.wait_ge(b.s_t.h, tv_kk[kk - depth] if kk >= depth else wave_t0)
                    for ihl in range(2):
                        b.s_ld.inc(eng.dma_start(
                            out=AW[:, (slot * 2 + ihl) * 1024:(slot * 2 + ihl + 1) * 1024],
                            in_=b.wqkv_s[kk, ihl, :, colbase:colbase + C]), 16)
                    b.sentinel(eng)
                self.on("sync", fsync)
                ldv = b.s_ld.n

                def ften(eng, kk=kk, wave=wave, ldv=ldv, vvx=vvx):
                    eng.wait_ge(b.s_ld.h, ldv)
                    eng.wait_ge(b.s_v.h, vvx)
                    if kk == 0:
                        eng.wait_ge(b.s_v.h, b.s_v.n)
                        eng.wait_ge(b.s_s.h, b.s_s.n)
                    slot = kk % 2

                    def wbuf(ihl):
                        return AW[:, (slot * 2 + ihl) * 1024:(slot * 2 + ihl + 1) * 1024]
                    last = None
                    if wave in ("q", "k"):
                        for g in range(NG):
                            for pi, (whl, xhl) in enumerate(PASSES3):
                                last = eng.matmul(out=bank8[g][:, :],
                                                  lhsT=wbuf(whl)[:, g * 128:(g + 1) * 128],
                                                  rhs=xnT(xhl, kk),
                                                  start=(kk == 0 and pi == 0),
                                                  stop=(kk == 7 and pi == 2))
                    else:
                        for j in range(4):
                            for pi, (whl, xhl) in enumerate(PASSES3):
                                for half in range(2):
                                    last = eng.matmul(
                                        out=bank8[j * 2 + half][:, :],
                                        lhsT=xnT(xhl, kk)[:, j * 128:(j + 1) * 128],
                                        rhs=wbuf(whl)[:, half * 512:(half + 1) * 512],
                                        start=(kk == 0 and pi == 0),
                                        stop=(kk == 7 and pi == 2))
                    b.s_t.inc(last)
                self.on("tensor", ften)
                tv_kk.append(b.s_t.n)
            tvw = b.s_t.n
            if wave in ("q", "k"):
                for g in range(NG):
                    def fscl(eng, g=g, wave=wave, tvw=tvw):
                        eng.wait_ge(b.s_t.h, tvw)
                        eng.wait_ge(b.s_v.h, b.s_v.n)
                        col = (TC_BQ if wave == "q" else TC_BK) + g
                        eng.activation(out=AFr[:, 2048:2560], in_=bank8[g][:, :],
                                       func=AF.Identity, bias=TAB[:, col:col + 1],
                                       scale=np.float32(1.0))
                        b.s_s.inc(eng.last)
                    self.on("scalar", fscl)
                    sv = b.s_s.n

                    def fvec(eng, g=g, c=c, wave=wave, sv=sv):
                        eng.wait_ge(b.s_s.h, sv)
                        q_ = AFr[:, 2048:2560]
                        sw = AFr[:, 2560:3072]
                        t1 = AFr[:, 3072:3584]
                        r32 = AFr[:, 3584:4096]
                        eng.stream_shuffle(out=sw, in_=q_, mask=SWAP_MASK)
                        eng.tensor_tensor(out=t1, in0=q_,
                                          in1=TAB[:, TC_COS + 512 * c:TC_COS + 512 * (c + 1)],
                                          op=OP.mult)
                        eng.tensor_tensor(out=sw, in0=sw,
                                          in1=TAB[:, TC_SIN + 512 * c:TC_SIN + 512 * (c + 1)],
                                          op=OP.mult)
                        eng.tensor_tensor(out=r32, in0=t1, in1=sw, op=OP.add)
                        base = 0 if wave == "q" else 8192
                        hi = AT[:, base + g * 512:base + (g + 1) * 512]
                        lo = AT[:, base + 4096 + g * 512:base + 4096 + (g + 1) * 512]
                        eng.tensor_copy(out=hi, in_=r32)
                        eng.tensor_tensor(out=lo, in0=r32, in1=hi, op=OP.subtract)
                        b.s_v.inc(eng.last)
                    self.on("vector", fvec)
                if wave == "k":
                    vvk = b.s_v.n

                    def fstk(eng, c=c, vvk=vvk):
                        eng.wait_ge(b.s_v.h, vvk)
                        for g in range(NG):
                            for ihl, base in [(0, 8192), (1, 8192 + 4096)]:
                                b.s_st.inc(eng.dma_start(
                                    out=b.kt_s[g, ihl, :, c * 512:(c + 1) * 512],
                                    in_=AT[:, base + g * 512:base + (g + 1) * 512]), 16)
                    self.on("gpsimd", fstk)
            else:
                for j in range(4):
                    def fvec(eng, j=j, tvw=tvw):
                        eng.wait_ge(b.s_t.h, tvw)
                        for half in range(2):
                            eng.tensor_tensor(
                                out=AACC[:, j * 1024 + half * 512:j * 1024 + (half + 1) * 512],
                                in0=bank8[j * 2 + half][:, :],
                                in1=AFr[:, BV_COL + half * 512:BV_COL + (half + 1) * 512],
                                op=OP.add)
                        b.s_v.inc(eng.last)
                    self.on("vector", fvec)
                vvv = b.s_v.n

                def fstv(eng, c=c, vvv=vvv):
                    eng.wait_ge(b.s_v.h, vvv)
                    dst = b.v_s[c * 512:(c + 1) * 512, :].rearrange(
                        "(j p) f -> j p f", p=128).transpose([1, 0, 2])
                    src = AACC[:, 0:4096].rearrange("p (j f) -> p j f", j=4)
                    b.s_st.inc(eng.dma_start(out=dst, in_=src), 16)
                self.on("gpsimd", fstv)
        self.psum_close(2)

    def _a2(self, bi, c):
        b = self
        AFr, AT, AXNT, TAB, AQ = b.AFr, b.AT, b.AXNT, b.TAB, b.AQ
        nr = 4 * c + 4
        pbc = self.psum_open("pbc", 512, F32)
        PT = [2048, 5632]

        def emit_av(g, r, tv_av, sv_r, vprev):
            rb = r % 2
            pb = PT[rb]
            sve = sv_r[r]

            def ften_av(eng, g=g, r=r, rb=rb, pb=pb, sve=sve,
                        first=(r == 0), last=(r == nr - 1), vprev=vprev):
                eng.wait_ge(b.s_s.h, sve)
                if first:
                    eng.wait_ge(b.s_v.h, vprev)
                vb = b.VBUF[:, rb * 128:(rb + 1) * 128]
                eng.matmul(out=b.ps[4][0:64, :], lhsT=vb[:, 0:64],
                           rhs=AFr[:, pb:pb + 512], start=first, stop=last,
                           tile_position=(0, 0), skip_group_check=True)
                eng.matmul(out=b.ps[4][64:128, :], lhsT=vb[:, 64:128],
                           rhs=AFr[:, pb + 512:pb + 1024], start=first, stop=last,
                           tile_position=(0, 64), skip_group_check=True)
                eng.matmul(out=b.ps[5][0:1, :], lhsT=b.ONESF[:, 0:1],
                           rhs=AFr[:, pb:pb + 512], start=first, stop=last,
                           tile_position=(0, 0), skip_group_check=True)
                b.s_t.inc(eng.matmul(out=b.ps[5][32:33, :], lhsT=b.ONESF[:, 1:2],
                           rhs=AFr[:, pb + 512:pb + 1024], start=first, stop=last,
                           tile_position=(0, 32), skip_group_check=True))
            self.on("tensor", ften_av)
            tv_av.append(b.s_t.n)

        for g in range(NG):
            if g == 1:
                self._a3_prefetch(bi, c)
            tv_av = []
            tv_s = []
            sv_r = []
            vprev = b.s_v.n
            for r in range(nr):
                rb = r % 2
                sb_bank = 2 * rb

                def fsync_r(eng, g=g, r=r, rb=rb):
                    eng.wait_ge(b.s_t.h, tv_av[r - 2] if r >= 2 else b.s_t.n)
                    for ihl in range(2):
                        b.s_ld.inc(eng.dma_start(
                            out=b.KBUF[:, rb * 512 + ihl * 128:rb * 512 + (ihl + 1) * 128],
                            in_=b.kt_s[g, ihl, :, r * 128:(r + 1) * 128]), 16)
                    b.s_ld.inc(eng.dma_start(
                        out=b.VBUF[:, rb * 128:(rb + 1) * 128],
                        in_=b.v_s[r * 128:(r + 1) * 128, 128 * g:128 * (g + 1)]), 16)
                self.on("sync", fsync_r)
                ldv = b.s_ld.n

                def ften_s(eng, g=g, r=r, rb=rb, sb_bank=sb_bank, ldv=ldv):
                    eng.wait_ge(b.s_ld.h, ldv)
                    if r >= 2:
                        eng.wait_ge(b.s_s.h, sv_r[r - 2])

                    def kb(ihl):
                        return b.KBUF[:, rb * 512 + ihl * 128:rb * 512 + (ihl + 1) * 128]
                    last = None
                    for pi, (khl, qhl) in enumerate(PASSES3):
                        qsl = AT[:, qhl * 4096 + g * 512:qhl * 4096 + (g + 1) * 512]
                        eng.matmul(out=b.ps[sb_bank][:, :], lhsT=kb(khl)[0:64, :],
                                   rhs=qsl[0:64, :], start=(pi == 0), stop=(pi == 2),
                                   tile_position=(0, 0))
                        last = eng.matmul(out=b.ps[sb_bank + 1][:, :], lhsT=kb(khl)[64:128, :],
                                          rhs=qsl[64:128, :], start=(pi == 0), stop=(pi == 2),
                                          tile_position=(64, 0))
                    b.s_t.inc(last)
                self.on("tensor", ften_s)
                tv_s.append(b.s_t.n)
                tvs = b.s_t.n

                jj = r - 4 * c
                mv = None
                if jj >= 0:
                    def fvec_m(eng, jj=jj, sb_bank=sb_bank, tvs=tvs):
                        eng.wait_ge(b.s_t.h, tvs)
                        w = (jj + 1) * 128
                        for bk in range(2):
                            eng.tensor_tensor(out=b.ps[sb_bank + bk][:, 0:w],
                                              in0=b.ps[sb_bank + bk][:, 0:w],
                                              in1=b.MSK[:, jj * 512:jj * 512 + w],
                                              op=OP.add)
                        b.s_v.inc(eng.last)
                    self.on("vector", fvec_m)
                    mv = b.s_v.n

                def fscl_e(eng, r=r, rb=rb, sb_bank=sb_bank, tvs=tvs, mv=mv):
                    eng.wait_ge(b.s_t.h, tvs)
                    if r >= 2:
                        eng.wait_ge(b.s_t.h, tv_av[r - 2])
                    if mv is not None:
                        eng.wait_ge(b.s_v.h, mv)
                    pb = PT[rb]
                    for bk in range(2):
                        eng.activation(out=AFr[:, pb + bk * 512:pb + (bk + 1) * 512],
                                       in_=b.ps[sb_bank + bk][:, :], func=AF.Exp,
                                       bias=np.float32(-4.0), scale=np.float32(0.125))
                    b.s_s.inc(eng.last)
                self.on("scalar", fscl_e)
                sv_r.append(b.s_s.n)
                if PIPE_A2:
                    if r >= 1:
                        emit_av(g, r - 1, tv_av, sv_r, vprev)
                else:
                    emit_av(g, r, tv_av, sv_r, vprev)
            if PIPE_A2:
                emit_av(g, nr - 1, tv_av, sv_r, vprev)
            tva = tv_av[-1]

            def fvec_d(eng, tva=tva):
                eng.wait_ge(b.s_t.h, tva)
                dn = TAB[:, TC_DEN:TC_DEN + 512]
                eng.tensor_copy(out=dn[0:1, :], in_=b.ps[5][0:1, :])
                eng.tensor_copy(out=dn[32:33, :], in_=b.ps[5][32:33, :])
                eng.reciprocal(out=dn[0:1, :], in_=dn[0:1, :])
                eng.reciprocal(out=dn[32:33, :], in_=dn[32:33, :])
                b.s_v.inc(eng.last)
            self.on("vector", fvec_d)
            vvd = b.s_v.n

            def ften_b(eng, vvd=vvd):
                eng.wait_ge(b.s_v.h, vvd)
                dn = TAB[:, TC_DEN:TC_DEN + 512]
                eng.matmul(out=pbc[0:64, :], lhsT=b.ONESF[0:1, 0:64],
                           rhs=dn[0:1, :], start=True, stop=True, tile_position=(0, 0))
                b.s_t.inc(eng.matmul(out=pbc[64:128, :], lhsT=b.ONESF[32:33, 0:64],
                           rhs=dn[32:33, :], start=True, stop=True,
                           tile_position=(32, 64), skip_group_check=True))
            self.on("tensor", ften_b)
            tvb = b.s_t.n

            def fvec_y(eng, g=g, tvb=tvb):
                eng.wait_ge(b.s_t.h, tvb)
                bc = AFr[:, 3072:3584]
                yn = AFr[:, 3584:4096]
                eng.tensor_copy(out=bc, in_=pbc[:, :])
                eng.tensor_tensor(out=yn, in0=b.ps[4][:, :], in1=bc, op=OP.mult)
                hi = AXNT[:, g * 512:(g + 1) * 512]
                lo = AXNT[:, 4096 + g * 512:4096 + (g + 1) * 512]
                eng.tensor_copy(out=hi, in_=yn)
                eng.tensor_tensor(out=lo, in0=yn, in1=hi, op=OP.subtract)
                b.s_v.inc(eng.last)
            self.on("vector", fvec_y)
        self.psum_close(1)

    def _a3_prefetch(self, bi, c):
        b = self

        def fsync0(eng, bi=bi, c=c):
            dst = b.AACC[:, 0:4096].rearrange("p (j f) -> p j f", j=4)
            src = b.x[bi, c * 512:(c + 1) * 512, :].rearrange(
                "(j p) f -> j p f", p=128).transpose([1, 0, 2])
            b.s_ld.inc(eng.dma_start(out=dst, in_=src), 16)
            b.sentinel(eng)
        self.on("sync", fsync0)
        self._a3_ldx = b.s_ld.n

    def _a3(self, bi, c):
        b = self
        AFr, AXNT, AACC, AW = b.AFr, b.AXNT, b.AACC, b.AW
        ldx = self._a3_ldx

        def fvec_x(eng, ldx=ldx):
            eng.wait_ge(b.s_ld.h, ldx)
            for j in range(4):
                eng.tensor_tensor(out=AACC[:, j * 1024:(j + 1) * 1024],
                                  in0=AACC[:, j * 1024:(j + 1) * 1024],
                                  in1=AFr[:, BP_COL:BP_COL + C], op=OP.add)
            b.s_v.inc(eng.last)
        self.on("vector", fvec_x)
        vvx = b.s_v.n
        pse = [self.psum_open("pa6", 512, F32), self.psum_open("pa7", 512, F32)]
        bank8 = b.ps + pse
        tv_g = []
        a3_t0 = b.s_t.n
        for g in range(NG):
            def fsync(eng, g=g):
                slot = g % 2
                depth = 2 if PIPE_A3 else 1
                eng.wait_ge(b.s_t.h, tv_g[g - depth] if g >= depth else a3_t0)
                for ihl in range(2):
                    b.s_ld.inc(eng.dma_start(
                        out=AW[:, (slot * 2 + ihl) * 1024:(slot * 2 + ihl + 1) * 1024],
                        in_=b.wproj_s[g, ihl, :, :]), 16)
                b.sentinel(eng)
            self.on("sync", fsync)
            ldv = b.s_ld.n

            def ften(eng, g=g, ldv=ldv, vvx=vvx):
                eng.wait_ge(b.s_ld.h, ldv)
                if g == 0:
                    eng.wait_ge(b.s_v.h, max(vvx, b.s_v.n))
                slot = g % 2
                last = None
                for j in range(4):
                    for half in range(2):
                        for pi, (whl, yhl) in enumerate(PASSES3):
                            last = eng.matmul(
                                out=bank8[j * 2 + half][:, :],
                                lhsT=AXNT[:, yhl * 4096 + g * 512 + j * 128:
                                          yhl * 4096 + g * 512 + (j + 1) * 128],
                                rhs=AW[:, (slot * 2 + whl) * 1024 + half * 512:
                                       (slot * 2 + whl) * 1024 + (half + 1) * 512],
                                start=(g == 0 and pi == 0),
                                stop=(g == NG - 1 and pi == 2))
                b.s_t.inc(last)
            self.on("tensor", ften)
            tv_g.append(b.s_t.n)
        tvp = b.s_t.n
        for j in range(4):
            def fvec(eng, j=j, tvp=tvp):
                eng.wait_ge(b.s_t.h, tvp)
                for half in range(2):
                    eng.tensor_tensor(
                        out=AACC[:, 4096 + j * 1024 + half * 512:
                                 4096 + j * 1024 + (half + 1) * 512],
                        in0=bank8[j * 2 + half][:, :],
                        in1=AACC[:, j * 1024 + half * 512:j * 1024 + (half + 1) * 512],
                        op=OP.add)
                b.s_v.inc(eng.last)
            self.on("vector", fvec)
        vvp = b.s_v.n

        def fst(eng, c=c, vvp=vvp):
            eng.wait_ge(b.s_v.h, vvp)
            dst = b.part_s[c * 512:(c + 1) * 512, :].rearrange(
                "(j p) f -> j p f", p=128).transpose([1, 0, 2])
            src = AACC[:, 4096:8192].rearrange("p (j f) -> p j f", j=4)
            b.s_st.inc(eng.dma_start(out=dst, in_=src), 16)
        self.on("gpsimd", fst)
        self.psum_close(2)

    def emit_phase_b(self, bi):
        b = self
        AFr, AT, AXNT, AACC, AW, AQ = b.AFr, b.AT, b.AXNT, b.AACC, b.AW, b.AQ
        self.trA = self.psum_open("trA", 128, BF16)
        self.trB = self.psum_open("trB", 128, BF16)
        for half in range(2):
            toff = half * 8

            def fsync_p(eng, toff=toff):
                dst = AACC[:, 0:8192].rearrange("p (j f) -> p j f", j=8)
                src = b.part_s[toff * 128:(toff + 8) * 128, :].rearrange(
                    "(j p) f -> j p f", p=128).transpose([1, 0, 2])
                b.s_ld.inc(eng.dma_start(out=dst, in_=src), 16)
                b.sentinel(eng)
            self.on("sync", fsync_p)
            ldp = b.s_ld.n
            for tj in range(8):
                t = toff + tj

                def fsync(eng, t=t):
                    eng.wait_ge(b.s_v.h, b.s_v.n)
                    b.s_ld.inc(eng.dma_start(out=AFr[:, 0:1024],
                                             in_=b.xn_s[t * 128:(t + 1) * 128, :]), 16)
                    b.sentinel(eng)
                self.on("sync", fsync)
                ldv = b.s_ld.n

                def fvw(eng, ldv=ldv):
                    eng.wait_ge(b.s_ld.h, ldv)
                self.on("vector", fvw)
                self.on("vector", self.emit_quant(C, AFr[:, 0:1024], tcol=1024,
                                                  hi_ap=AQ[:, 0:1024],
                                                  lo_ap=AQ[:, 1024:2048]))

                def fv2(eng):
                    b.s_v.inc(eng.last)
                self.on("vector", fv2)
                vv = b.s_v.n

                pairs = []
                for ihl in range(2):
                    for kk in range(8):
                        pairs.append((AQ[:, ihl * 1024 + kk * 128:ihl * 1024 + (kk + 1) * 128],
                                      AT[:, ihl * 8192 + kk * 1024 + tj * 128:
                                           ihl * 8192 + kk * 1024 + (tj + 1) * 128]))
                self.emit_tr_blocks(pairs,
                                    pre_wait=lambda eng, vv=vv: eng.wait_ge(b.s_v.h, max(vv, b.s_v.n)))
            self.barrier()
            icol = 0
            n_tile = 0
            h_done = []
            pend = None
            GU = [(0, 1), (4, 5)]

            def emit_tail(pend, facc_only=False):
                (tj, IWp, nsubp, wd_bp, vq, ldw_p) = pend
                par = (n_tile - 1) % 2
                pairs = []
                for ihl in range(2):
                    for s in range(nsubp):
                        pairs.append((AQ[:, ihl * 512 + s * 128:ihl * 512 + (s + 1) * 128],
                                      AXNT[:, (ihl * nsubp + s) * 128:
                                           (ihl * nsubp + s + 1) * 128]))
                self.emit_tr_blocks(pairs,
                                    pre_wait=lambda eng, vq=vq: eng.wait_ge(b.s_v.h, vq))
                vve = b.s_v.n

                def hqT(ihl, s, nsubp=nsubp):
                    return AXNT[:, (ihl * nsubp + s) * 128:(ihl * nsubp + s + 1) * 128]

                def ften_d(eng, nsubp=nsubp, vve=vve, wd_bp=wd_bp, ldw_p=ldw_p):
                    eng.wait_ge(b.s_v.h, vve)
                    eng.wait_ge(b.s_ld.h, ldw_p)
                    last = None
                    for s in range(nsubp):
                        for pi, (hhl, whl) in enumerate(PASSES3):
                            for ch in range(2):
                                last = eng.matmul(out=b.ps[2 + ch][:, :],
                                           lhsT=hqT(hhl, s),
                                           rhs=wd_bp(s, whl)[:, ch * 512:(ch + 1) * 512],
                                           start=(s == 0 and pi == 0),
                                           stop=(s == nsubp - 1 and pi == 2))
                    b.s_t.inc(last)
                self.on("tensor", ften_d)
                tvd = b.s_t.n

                def facc(eng, tj=tj, tvd=tvd):
                    eng.wait_ge(b.s_t.h, tvd)
                    for ch in range(2):
                        a = AACC[:, tj * 1024 + ch * 512:tj * 1024 + (ch + 1) * 512]
                        eng.tensor_tensor(out=a, in0=a, in1=b.ps[2 + ch][:, :],
                                          op=OP.add)
                    b.s_v.inc(eng.last)
                self.on("vector", facc)
                return b.s_v.n

            prev_tv_gu = None
            for ic, IW in enumerate(I_CHUNKS):
                nsub = IW // 128
                if pend is not None:
                    emit_tail(pend)
                    pend = None

                def fsync1(eng, icol=icol, IW=IW,
                           gate=prev_tv_gu):
                    eng.wait_ge(b.s_t.h, gate if gate is not None else b.s_t.n)
                    for kh in range(2):
                        for wi, w_s in [(0, b.wg_s), (1, b.wu_s)]:
                            for a in range(2):
                                src_ = w_s[a, kh * 4:kh * 4 + 4, :,
                                           icol:icol + IW].transpose([1, 0, 2])
                                base = wi * 4096 + a * 8 * IW + kh * 4 * IW
                                dst = AW[:, base:base + 4 * IW].rearrange(
                                    "p (k q) -> p k q", k=4)
                                b.s_ld.inc(eng.dma_start(out=dst, in_=src_), 16)
                        if kh == 0:
                            ld_half[0] = b.s_ld.n
                ld_half = [0]
                self.on("sync", fsync1)
                ldv = b.s_ld.n
                ldv1 = ld_half[0]

                def fsync2(eng, icol=icol, nsub=nsub):
                    eng.wait_ge(b.s_t.h, b.s_t.n)
                    sub0 = icol // 128
                    src_ = b.wd_s[sub0:sub0 + nsub, :, :, :].transpose([2, 0, 1, 3])
                    dst = AW[:, 8192:8192 + nsub * 2048].rearrange(
                        "p (s a q) -> p s a q", s=nsub, a=2)
                    b.s_ld.inc(eng.dma_start(out=dst, in_=src_), 16)
                    b.sentinel(eng)
                self.on("sync", fsync2)
                ldw = b.s_ld.n

                def wgu_b(wi, ihl, kk, IW=IW):
                    off = wi * 4096 + (ihl * 8 + kk) * IW
                    return AW[:, off:off + IW]

                def wd_b(s, ihl):
                    off = 8192 + (s * 2 + ihl) * 1024
                    return AW[:, off:off + 1024]
                for tj in range(8):
                    par = n_tile % 2
                    g0, g1 = GU[par]
                    sil = par * 512
                    hcol = 1024 + par * 256

                    def ften_gu(eng, tj=tj, IW=IW, ldv=ldv, ldv1=ldv1, ldp=ldp,
                                g0=g0, g1=g1,
                                hfree=(h_done[n_tile - 2] if n_tile >= 2 else 0)):
                        eng.wait_ge(b.s_ld.h, max(ldv1, ldp))
                        eng.wait_ge(b.s_v.h, hfree)
                        last = None
                        for kk in range(8):
                            if kk == 4:
                                eng.wait_ge(b.s_ld.h, ldv)
                            for pi, (whl, xhl) in enumerate(PASSES3):
                                lhs = AT[:, xhl * 8192 + kk * 1024 + tj * 128:
                                         xhl * 8192 + kk * 1024 + (tj + 1) * 128]
                                eng.matmul(out=b.ps[g0][:, 0:IW], lhsT=lhs,
                                           rhs=wgu_b(0, whl, kk),
                                           start=(kk == 0 and pi == 0),
                                           stop=(kk == 7 and pi == 2))
                                last = eng.matmul(out=b.ps[g1][:, 0:IW], lhsT=lhs,
                                                  rhs=wgu_b(1, whl, kk),
                                                  start=(kk == 0 and pi == 0),
                                                  stop=(kk == 7 and pi == 2))
                        b.s_t.inc(last)
                    self.on("tensor", ften_gu)
                    tv = b.s_t.n

                    def fscl(eng, IW=IW, tv=tv, g0=g0, sil=sil):
                        eng.wait_ge(b.s_t.h, tv)
                        eng.activation(out=AFr[:, sil:sil + IW], in_=b.ps[g0][:, 0:IW],
                                       func=AF.Silu, bias=np.float32(0.0),
                                       scale=np.float32(1.0))
                        b.s_s.inc(eng.last)
                    self.on("scalar", fscl)
                    sv = b.s_s.n

                    if pend is not None:
                        emit_tail(pend)
                        pend = None

                    def fvec_h(eng, IW=IW, sv=sv, g1=g1, sil=sil, hcol=hcol):
                        eng.wait_ge(b.s_s.h, sv)
                        eng.tensor_tensor(out=AFr[:, hcol:hcol + IW],
                                          in0=AFr[:, sil:sil + IW],
                                          in1=b.ps[g1][:, 0:IW], op=OP.mult)
                    self.on("vector", fvec_h)
                    self.on("vector", self.emit_quant(IW, AFr[:, hcol:hcol + IW],
                                                      tcol=2048,
                                                      hi_ap=AQ[:, 0:IW],
                                                      lo_ap=AQ[:, 512:512 + IW]))

                    def fv2(eng):
                        b.s_v.inc(eng.last)
                    self.on("vector", fv2)
                    h_done.append(b.s_v.n)
                    n_tile += 1
                    pend = (tj, IW, nsub, wd_b, b.s_v.n, ldw)
                    if (not PIPE_B) or ic == len(I_CHUNKS) - 1:
                        vvf = emit_tail(pend)
                        pend = None
                    if ic == len(I_CHUNKS) - 1:

                        def fout(eng, bi=bi, toff=toff, tj=tj, vvf=vvf):
                            eng.wait_ge(b.s_v.h, vvf)
                            b.s_st.inc(eng.dma_start(
                                out=b.out[bi, (toff + tj) * 128:(toff + tj + 1) * 128, :],
                                in_=AACC[:, tj * 1024:(tj + 1) * 1024]), 16)
                        self.on("gpsimd", fout)
                prev_tv_gu = tv
                icol += IW
            self.barrier()
        self.psum_close(2)

    def emit_scratch_dump(self):
        b = self

        def fd(eng):
            eng.wait_ge(b.s_v.h, b.s_v.n)
            eng.wait_ge(b.s_s.h, b.s_s.n)
            eng.wait_ge(b.s_t.h, b.s_t.n)
            eng.wait_ge(b.s_st.h, b.s_st.n)
            b.s_st.inc(eng.dma_start(out=b.dbg2[0], in_=b.xn_s[:, :]), 16)
            b.s_st.inc(eng.dma_start(out=b.dbg2[1], in_=b.part_s[:, :]), 16)
            b.s_st.inc(eng.dma_start(out=b.dbg2[2], in_=b.v_s[:, :]), 16)
            b.s_st.inc(eng.dma_start(out=b.dbg2[3, 0:128, :],
                                     in_=b.kt_s[0, 0, :, :].bitcast(F32)), 16)
        self.on("gpsimd", fd)

    def emit_debug_dump(self):
        b = self
        def fd(eng):
            eng.wait_ge(b.s_v.h, b.s_v.n)
            eng.wait_ge(b.s_s.h, b.s_s.n)
            eng.wait_ge(b.s_t.h, b.s_t.n)
            b.s_st.inc(eng.dma_start(out=b.dbg[0, :, 0:8192],
                                     in_=b.AT.ap().bitcast(F32)), 16)
            b.s_st.inc(eng.dma_start(out=b.dbg[1, :, 0:4096],
                                     in_=b.AXNT.ap().bitcast(F32)), 16)
            b.s_st.inc(eng.dma_start(out=b.dbg[2, :, 0:8192], in_=b.AFr[:, 0:8192]), 16)
        self.on("gpsimd", fd)

    def finish(self):
        b = self
        stv = b.s_st.n

        def fdone(eng, stv=stv):
            eng.wait_ge(b.s_st.h, stv)
        self.on("gpsimd", fdone)
        self.stack.close()
        return self.nc


def build_nc(stage="full"):
    b = Builder()
    b.emit_setup()
    if stage != "setup":
        b.emit_weight_prep()
        b.emit_bias_setup()
        b.barrier()
    if stage in ("full", "a1", "a2", "a3", "b1", "a_all", "ab1", "f2a"):
        nb = 2 if stage in ("full", "f2a") else 1
        nch = NCH if stage in ("full", "a_all", "ab1", "f2a") else 1
        for bi in range(nb):
            if stage != "b1":
                for c in range(nch):
                    b._a1(bi, c)
                    b.barrier()
                    if stage == "a1":
                        b.emit_debug_dump()
                        break
                    b._a2(bi, c)
                    b.barrier()
                    if stage == "a2":
                        b.emit_debug_dump()
                        break
                    b._a3(bi, c)
                    b.barrier()
            if stage in ("a_all",):
                b.emit_scratch_dump()
            if stage in ("full", "b1", "ab1") or (stage == "f2a" and bi == 0):
                b.emit_phase_b(bi)
                b.barrier()
                if stage == "ab1":
                    b.emit_scratch_dump()
    if stage == "full" and DEBUG:
        b.emit_scratch_dump()
    return b.finish()


def kernel(L0_attn_x, L0_attn_W_qkv, L0_attn_b_qkv, L0_attn_W_proj, L0_attn_b_proj,
           L0_ffn_w_gate, L0_ffn_w_up, L0_ffn_w_down):
    x = np.ascontiguousarray(np.asarray(L0_attn_x, dtype=np.float32))
    tabs = _host_tables()
    btabs = _host_bias_tables(np.asarray(L0_attn_b_qkv, dtype=np.float32),
                              np.asarray(L0_attn_b_proj, dtype=np.float32))
    nc = build_nc()
    base = {
        "wqkv": np.ascontiguousarray(np.asarray(L0_attn_W_qkv, dtype=np.float32)),
        "wproj": np.ascontiguousarray(np.asarray(L0_attn_W_proj, dtype=np.float32)),
        "wg_in": np.ascontiguousarray(np.asarray(L0_ffn_w_gate, dtype=np.float32)),
        "wu_in": np.ascontiguousarray(np.asarray(L0_ffn_w_up, dtype=np.float32)),
        "wd_in": np.ascontiguousarray(np.asarray(L0_ffn_w_down, dtype=np.float32)),
        **tabs, **btabs,
    }
    in_maps = []
    for core in range(8):
        m = dict(base)
        m["x"] = np.ascontiguousarray(x[2 * core:2 * core + 2])
        in_maps.append(m)
    res = run_bass_kernel_spmd(nc, in_maps, core_ids=list(range(8)))
    out = np.concatenate([np.asarray(r["out"]) for r in res.results], axis=0)
    return out.astype(np.float32)


# version 11 (final)
# speedup vs baseline: 1.0158x; 1.0080x over previous
import contextlib
import numpy as np
import ml_dtypes
import concourse.bass as bass
import concourse.mybir as mybir
from concourse.bass_utils import run_bass_kernel_spmd
from concourse.alu_op_type import AluOpType as OP

F32 = mybir.dt.float32
BF16 = mybir.dt.bfloat16
AF = mybir.ActivationFunctionType
BF = ml_dtypes.bfloat16
f32 = np.float32

B_PER_CORE = 2
T = 2048
C = 1024
NG = 8
I_FF = 2816
NT = T // 128
NCH = T // 512
I_CHUNKS = [256] * 11
PASSES3 = [(0, 0), (0, 1), (1, 0)]
SWAP_MASK = [i ^ 1 for i in range(32)]
A1_WAVES = [("q", 0), ("k", C), ("v", 2 * C)]
PREP_PARTS = ("qkv", "proj", "wg", "wu", "wd")
WG_MODE = "full"
DEBUG = False
PIPE_A1 = True
PIPE_A2 = True
PIPE_A3 = True
PIPE_B = True

AFR_W = 8704
BV_COL = 6656
BP_COL = 7680
TC_COS = 0
TC_SIN = 2048
TC_BQ = 4096
TC_BK = 4104
TC_ST = 4112
TC_DEN = 4120
TAB_W = 4640


def _host_tables():
    import jax
    import jax.numpy as jnp
    t = {}
    with jax.default_device(jax.devices("cpu")[0]):
        pos = jnp.arange(T, dtype=jnp.float32)[:, None]
        i = jnp.arange(1, 33, dtype=jnp.float32)[None]
        theta = 1.0 / 10000.0 ** (2.0 * (i - 1.0) / 64.0)
        ang = pos * theta
        cosv = np.asarray(jnp.cos(ang), dtype=np.float64)
        sinv = np.asarray(jnp.sin(ang), dtype=np.float64)
    p = np.arange(128)
    jrow = (p % 64) // 2
    sgn = np.where(p % 2 == 0, -1.0, 1.0)
    cosT = cosv[:, jrow].T
    sinT = sinv[:, jrow].T * sgn[:, None]
    t["cos_h"] = cosT.astype(f32)
    t["cos_l"] = np.zeros_like(t["cos_h"])
    t["sin_h"] = sinT.astype(f32)
    t["sin_l"] = np.zeros_like(t["sin_h"])
    m = np.zeros((4, 128, 512), dtype=f32)
    for jj in range(4):
        kt = 128 * jj + np.arange(128)[:, None]
        q = np.arange(512)[None, :]
        m[jj] = np.where(kt > q, -1e9, 0.0)
    t["masks"] = m.astype(BF)
    t["ident"] = np.eye(128, dtype=BF)
    return t


def _host_bias_tables(b_qkv, b_proj):
    t = {}
    p = np.arange(128)
    bq = np.zeros((128, NG), dtype=f32)
    bk = np.zeros((128, NG), dtype=f32)
    for g in range(NG):
        head = 2 * g + p // 64
        d = p % 64
        bq[:, g] = b_qkv[192 * head + d]
        bk[:, g] = b_qkv[192 * head + 64 + d]
    t["bq_tab"] = bq
    t["bk_tab"] = bk
    jcol = np.arange(C)
    bv = b_qkv[192 * (jcol // 64) + 128 + (jcol % 64)]
    t["bv_bc"] = np.broadcast_to(bv.astype(f32), (128, C)).copy()
    t["bp_bc"] = np.broadcast_to(b_proj.astype(f32), (128, C)).copy()
    return t


class _EngP:
    _NOTRACK = {"wait_ge", "drain", "nop", "engine_nop", "sem_inc"}

    def __init__(self, eng, store, key):
        object.__setattr__(self, "_e", eng)
        object.__setattr__(self, "_store", store)
        object.__setattr__(self, "_key", key)

    @property
    def last(self):
        return self._store[self._key]

    def __getattr__(self, n):
        f = getattr(self._e, n)
        if callable(f) and not n.startswith("_"):
            track = n not in _EngP._NOTRACK
            store, key = self._store, self._key

            def w(*a, **k):
                r = f(*a, **k)
                if track and hasattr(r, "then_inc"):
                    store[key] = r
                return r
            return w
        return f


class Sem:
    def __init__(self, stack, nc, name):
        self.h = stack.enter_context(nc.semaphore(name))
        self.n = 0

    def inc(self, instr, k=1):
        instr.then_inc(self.h, k)
        self.n += k
        return self.n


class Builder:
    def __init__(self):
        self.nc = bass.Bass("TRN2", target_bir_lowering=False,
                            detect_race_conditions=False)
        nc = self.nc

        def dram(n, s, d, k=None):
            h = nc.dram_tensor(n, s, d, kind=k) if k else nc.dram_tensor(n, s, d)
            return h.ap()
        self.x = dram("x", [B_PER_CORE, T, C], F32, "ExternalInput")
        self.wqkv = dram("wqkv", [C, 3 * C], F32, "ExternalInput")
        self.wproj = dram("wproj", [C, C], F32, "ExternalInput")
        self.wg_in = dram("wg_in", [I_FF, C], F32, "ExternalInput")
        self.wu_in = dram("wu_in", [I_FF, C], F32, "ExternalInput")
        self.wd_in = dram("wd_in", [C, I_FF], F32, "ExternalInput")
        for n in ["cos_h", "cos_l", "sin_h", "sin_l"]:
            setattr(self, n, dram(n, [128, T], F32, "ExternalInput"))
        self.masks = dram("masks", [4, 128, 512], BF16, "ExternalInput")
        self.ident = dram("ident", [128, 128], BF16, "ExternalInput")
        self.bq_tab = dram("bq_tab", [128, NG], F32, "ExternalInput")
        self.bk_tab = dram("bk_tab", [128, NG], F32, "ExternalInput")
        self.bv_bc = dram("bv_bc", [128, C], F32, "ExternalInput")
        self.bp_bc = dram("bp_bc", [128, C], F32, "ExternalInput")
        self.out = dram("out", [B_PER_CORE, T, C], F32, "ExternalOutput")
        self.wqkv_s = dram("wqkv_s", [8, 2, 128, 3 * C], BF16)
        self.wproj_s = dram("wproj_s", [8, 2, 128, C], BF16)
        self.wg_s = dram("wg_s", [2, 8, 128, I_FF], BF16)
        self.wu_s = dram("wu_s", [2, 8, 128, I_FF], BF16)
        self.wd_s = dram("wd_s", [22, 2, 128, C], BF16)
        self.xn_s = dram("xn_s", [T, C], F32)
        self.kt_s = dram("kt_s", [NG, 2, 128, T], BF16)
        self.v_s = dram("v_s", [T, C], F32)
        self.part_s = dram("part_s", [T, C], F32)
        if DEBUG:
            self.dbg = dram("dbg", [3, 128, 16384], F32)
            self.dbg2 = dram("dbg2", [4, T, C], F32, "ExternalOutput")

        self.stack = contextlib.ExitStack()
        ec = self.stack.enter_context

        def sb(n, cols, dt):
            return ec(nc.sbuf_tensor(n, [128, cols], dt))
        self.AW = sb("AW", 12544, BF16)
        self.AT = sb("AT", 16384, BF16)
        self.AXNT = sb("AXNT", 8192, BF16)
        self.AACC = sb("AACC", 8192, F32)
        self.AFr = sb("AF", AFR_W, F32)
        self.AQ = sb("AQ", 5664, BF16)
        self.TAB = sb("TAB", TAB_W, F32)
        self.MSK = sb("MSK", 2048, BF16)
        self.KBUF = sb("KBUF", 1024, BF16)
        self.VBUF = sb("VBUF", 512, F32)
        self.ID16 = sb("ID16", 128, BF16)
        self.ID32 = sb("ID32", 128, F32)
        self.ONES16 = sb("ONES16", 16, BF16)
        self.ONESF = sb("ONESF", 64, F32)
        self.ps = [ec(nc.psum_tensor(f"pb{i}", [128, 512], F32)) for i in range(6)]
        self._psum_scoped = []

        self.s_bar = Sem(self.stack, nc, "s_bar")
        self.s_ld = Sem(self.stack, nc, "s_ld")
        self.s_st = Sem(self.stack, nc, "s_st")
        self.s_t = Sem(self.stack, nc, "s_t")
        self.s_v = Sem(self.stack, nc, "s_v")
        self.s_s = Sem(self.stack, nc, "s_s")
        self.block = ec(nc.Block())

    def on(self, eng_name, fn):
        if not hasattr(self, "_lasts"):
            self._lasts = {}
        self._lasts.setdefault(eng_name, None)

        def wrapped(eng):
            fn(_EngP(eng, self._lasts, eng_name))
        getattr(self.block, eng_name)(wrapped)

    def psum_open(self, name, cols, dt):
        self._psn = getattr(self, "_psn", 0) + 1
        g = self.nc.psum_tensor(f"{name}{self._psn}", [128, cols], dt)
        t = g.__enter__()
        self._psum_scoped.append(g)
        return t

    def psum_close(self, n=1):
        for _ in range(n):
            g = self._psum_scoped.pop()
            g.__exit__(None, None, None)

    def sentinel(self, eng):
        self.s_ld.inc(eng.dma_start(out=self.TAB[:, 4632:4640], in_=self.bq_tab), 16)

    def emit_tr_blocks(self, pairs, pre_wait=None):
        b = self
        t0 = b.s_t.n
        v0 = b.s_v.n
        banks = [self.trA, self.trB]

        def ften(eng):
            if pre_wait is not None:
                pre_wait(eng)
            for i, (s, d) in enumerate(pairs):
                if i >= 2:
                    eng.wait_ge(b.s_v.h, v0 + i - 1)
                b.s_t.inc(eng.transpose(out=banks[i % 2][:, :], in_=s,
                                        identity=b.ID16[:, :]))
        self.on("tensor", ften)

        def fvec(eng):
            for i, (s, d) in enumerate(pairs):
                eng.wait_ge(b.s_t.h, t0 + i + 1)
                b.s_v.inc(eng.tensor_copy(out=d, in_=banks[i % 2][:, :]))
        self.on("vector", fvec)

    def barrier(self):
        vals = [(self.s_t, self.s_t.n), (self.s_v, self.s_v.n),
                (self.s_s, self.s_s.n), (self.s_st, self.s_st.n),
                (self.s_ld, self.s_ld.n)]
        for e in ["sync", "gpsimd", "tensor", "vector", "scalar"]:
            def fn(eng, vals=vals):
                for s, n in vals:
                    eng.wait_ge(s.h, n)
            self.on(e, fn)

    def emit_setup(self):
        b = self
        TAB, AFr = b.TAB, b.AFr

        def fsync(eng):
            for src, dst in [(b.cos_h, AFr[:, 0:2048]), (b.cos_l, AFr[:, 2048:4096]),
                             (b.sin_h, AFr[:, 4096:6144]), (b.sin_l, AFr[:, 6144:8192])]:
                b.s_ld.inc(eng.dma_start(out=dst, in_=src), 16)
            b.s_ld.inc(eng.dma_start(out=TAB[:, TC_BQ:TC_BQ + NG], in_=b.bq_tab), 16)
            b.s_ld.inc(eng.dma_start(out=TAB[:, TC_BK:TC_BK + NG], in_=b.bk_tab), 16)
            b.s_ld.inc(eng.dma_start(
                out=b.MSK[:, 0:2048].rearrange("p (m q) -> p m q", m=4),
                in_=b.masks.transpose([1, 0, 2])), 16)
            b.s_ld.inc(eng.dma_start(out=b.ID16[:, :], in_=b.ident), 16)
        self.on("sync", fsync)
        ldv = b.s_ld.n

        def fvec(eng, ldv=ldv):
            eng.wait_ge(b.s_ld.h, ldv)
            eng.tensor_tensor(out=TAB[:, TC_COS:TC_COS + 2048],
                              in0=AFr[:, 0:2048], in1=AFr[:, 2048:4096], op=OP.add)
            eng.tensor_tensor(out=TAB[:, TC_SIN:TC_SIN + 2048],
                              in0=AFr[:, 4096:6144], in1=AFr[:, 6144:8192], op=OP.add)
            eng.memset(b.ONES16[:, :], 1.0)
            eng.memset(b.ONESF[:, :], 1.0)
            eng.tensor_copy(out=b.ID32[:, :], in_=b.ID16[:, :])
            b.s_v.inc(eng.last)
        self.on("vector", fvec)

    def emit_quant(self, F, x_ap, tcol, scol=5640, hi_ap=None, lo_ap=None):
        b = self
        A = b.AFr
        nb = F // 16
        S = A[:, scol:scol + nb]
        R = A[:, scol + nb:scol + 2 * nb]
        t0 = A[:, tcol:tcol + F]
        t1 = A[:, tcol + F:tcol + 2 * F]
        t2 = A[:, tcol + 2 * F:tcol + 3 * F]
        t3 = A[:, tcol + 3 * F:tcol + 4 * F]

        def fn(eng):
            eng.tensor_reduce(out=S.rearrange("p (n o) -> p n o", o=1),
                              in_=x_ap.rearrange("p (n s) -> p n s", s=16),
                              axis=mybir.AxisListType.X, op=OP.max,
                              apply_absolute_value=True)
            eng.drain()
            eng.tensor_scalar(out=S, in0=S, scalar1=1.0 / 6.0, scalar2=1e-12,
                              op0=OP.mult, op1=OP.max)
            eng.drain()
            eng.reciprocal(out=R, in_=S)
            eng.drain()
            Rb = R.rearrange("p (n o) -> p n o", o=1).broadcast_to((128, nb, 16))
            Sb = S.rearrange("p (n o) -> p n o", o=1).broadcast_to((128, nb, 16))
            x3 = x_ap.rearrange("p (n s) -> p n s", s=16)
            eng.tensor_tensor(out=t0.rearrange("p (n s) -> p n s", s=16),
                              in0=x3, in1=Rb, op=OP.mult)
            eng.scalar_tensor_tensor(out=t1, in0=t0, scalar=-1.0, in1=t0,
                                     op0=OP.mult, op1=OP.max)
            eng.tensor_scalar(out=t2, in0=t1, scalar1=0.25, scalar2=None, op0=OP.is_gt)
            for thr in (0.75, 1.25, 1.75):
                eng.scalar_tensor_tensor(out=t2, in0=t1, scalar=thr, in1=t2,
                                         op0=OP.is_gt, op1=OP.add)
            eng.tensor_scalar(out=t3, in0=t1, scalar1=2.5, scalar2=None, op0=OP.is_gt)
            eng.scalar_tensor_tensor(out=t3, in0=t1, scalar=3.5, in1=t3,
                                     op0=OP.is_gt, op1=OP.add)
            eng.tensor_scalar(out=t1, in0=t1, scalar1=5.0, scalar2=None, op0=OP.is_gt)
            eng.scalar_tensor_tensor(out=t3, in0=t1, scalar=2.0, in1=t3,
                                     op0=OP.mult, op1=OP.add)
            eng.scalar_tensor_tensor(out=t2, in0=t2, scalar=0.5, in1=t3,
                                     op0=OP.mult, op1=OP.add)
            eng.tensor_scalar(out=t1, in0=t0, scalar1=0.0, scalar2=None, op0=OP.is_lt)
            eng.tensor_tensor(out=t1, in0=t2, in1=t1, op=OP.mult)
            eng.scalar_tensor_tensor(out=t2, in0=t1, scalar=-2.0, in1=t2,
                                     op0=OP.mult, op1=OP.add)
            eng.tensor_tensor(out=x_ap.rearrange("p (n s) -> p n s", s=16),
                              in0=t2.rearrange("p (n s) -> p n s", s=16),
                              in1=Sb, op=OP.mult)
            if hi_ap is not None:
                eng.tensor_copy(out=hi_ap, in_=x_ap)
                eng.tensor_tensor(out=lo_ap, in0=x_ap, in1=hi_ap, op=OP.subtract)
        return fn

    def emit_weight_prep(self):
        b = self
        AFr, AW, AQ = b.AFr, b.AW, b.AQ
        for name, w_in, w_s, width in [("qkv", b.wqkv, b.wqkv_s, 3 * C),
                                       ("proj", b.wproj, b.wproj_s, C)]:
            if name not in PREP_PARTS:
                continue
            sec_v0, sec_st0 = b.s_v.n, b.s_st.n
            fv_n, st_n = [], []
            for kk in range(8):
                par = kk % 2
                src = AFr[:, par * 3072:par * 3072 + width]
                hi = AW[:, par * 3072:par * 3072 + width]
                lo = AW[:, 6272 + par * 3072:6272 + par * 3072 + width]

                def fsync(eng, kk=kk, w_in=w_in, src=src,
                          wv=(fv_n[kk - 2] if kk >= 2 else sec_v0)):
                    eng.wait_ge(b.s_v.h, wv)
                    b.s_ld.inc(eng.dma_start(out=src,
                                             in_=w_in[kk * 128:(kk + 1) * 128, :]), 16)
                    b.sentinel(eng)
                self.on("sync", fsync)
                ldv = b.s_ld.n

                def fvec(eng, ldv=ldv, src=src, hi=hi, lo=lo,
                         ws=(st_n[kk - 2] if kk >= 2 else sec_st0)):
                    eng.wait_ge(b.s_ld.h, ldv)
                    eng.wait_ge(b.s_st.h, ws)
                    eng.tensor_copy(out=hi, in_=src)
                    eng.tensor_tensor(out=lo, in0=src, in1=hi, op=OP.subtract)
                    b.s_v.inc(eng.last)
                self.on("vector", fvec)
                fv_n.append(b.s_v.n)
                vv = b.s_v.n

                def fst(eng, kk=kk, name=name, w_s=w_s, width=width, vv=vv,
                        hi=hi, lo=lo):
                    eng.wait_ge(b.s_v.h, vv)
                    for ihl, buf in [(0, hi), (1, lo)]:
                        if name == "qkv":
                            for sect in range(3):
                                src_ = buf.rearrange(
                                    "p (h s d) -> p h s d", h=16, s=3)[:, :, sect, :]
                                dst = w_s[kk, ihl, :, sect * C:(sect + 1) * C].rearrange(
                                    "p (h d) -> p h d", h=16)
                                b.s_st.inc(eng.dma_start(out=dst, in_=src_), 16)
                        else:
                            b.s_st.inc(eng.dma_start(out=w_s[kk, ihl, :, :],
                                                     in_=buf), 16)
                self.on("gpsimd", fst)
                st_n.append(b.s_st.n)
        self.trA = self.psum_open("trA", 128, BF16)
        self.trB = self.psum_open("trB", 128, BF16)
        for wname, w_in, w_s in [("wg", b.wg_in, b.wg_s), ("wu", b.wu_in, b.wu_s)]:
            if wname not in PREP_PARTS:
                continue
            sec_v0, sec_st0, sec_t0 = b.s_v.n, b.s_st.n, b.s_t.n
            qv_n, tv_n, st_n = [], [], []
            for it in range(22):
                par = it % 2
                src = AFr[:, 5824:5824 + C] if par == 0 else AFr[:, 4608:4608 + C]
                hl = AQ if par == 0 else AW
                hi, lo = hl[:, 0:C], hl[:, 1024:2048]
                trn = hl[:, 2048:4096]

                def fsync(eng, w_in=w_in, it=it, src=src,
                          wv=(qv_n[it - 2] if it >= 2 else sec_v0)):
                    eng.wait_ge(b.s_v.h, wv)
                    b.s_ld.inc(eng.dma_start(out=src,
                                             in_=w_in[it * 128:(it + 1) * 128, :]), 16)
                    b.sentinel(eng)
                self.on("sync", fsync)
                ldv = b.s_ld.n

                def fvw(eng, ldv=ldv, it=it,
                        wt=(tv_n[it - 2] if it >= 2 else sec_t0)):
                    eng.wait_ge(b.s_ld.h, ldv)
                    eng.wait_ge(b.s_t.h, wt)
                    if it < 2:
                        eng.wait_ge(b.s_st.h, sec_st0)
                self.on("vector", fvw)
                self.on("vector", self.emit_quant(C, src, tcol=0,
                                                  hi_ap=hi, lo_ap=lo))

                def fv2(eng):
                    b.s_v.inc(eng.last)
                self.on("vector", fv2)
                qv_n.append(b.s_v.n)
                vv = b.s_v.n

                pairs = []
                for ihl in range(2):
                    for kk in range(8):
                        pairs.append((hl[:, ihl * 1024 + kk * 128:ihl * 1024 + (kk + 1) * 128],
                                      trn[:, kk * 128 + ihl * 1024:
                                           kk * 128 + ihl * 1024 + 128]))
                ws = st_n[it - 2] if it >= 2 else sec_st0
                self.emit_tr_blocks(
                    pairs, pre_wait=lambda eng, vv=vv, ws=ws: (
                        eng.wait_ge(b.s_v.h, vv), eng.wait_ge(b.s_st.h, ws)))
                tv_n.append(b.s_t.n)
                vv2 = b.s_v.n

                def fst(eng, it=it, w_s=w_s, vv2=vv2, trn=trn):
                    eng.wait_ge(b.s_v.h, vv2)
                    for ihl in range(2):
                        dst = w_s[ihl, :, :, it * 128:(it + 1) * 128].transpose([1, 0, 2])
                        src_ = trn[:, ihl * 1024:(ihl + 1) * 1024].rearrange(
                            "p (k q) -> p k q", k=8)
                        b.s_st.inc(eng.dma_start(out=dst, in_=src_), 16)
                self.on("gpsimd", fst)
                st_n.append(b.s_st.n)
        sec_v0, sec_st0, sec_t0 = b.s_v.n, b.s_st.n, b.s_t.n
        qv_n, tv_n, st_n = [], [], []
        for ht in (range(8) if "wd" in PREP_PARTS else []):
            par = ht % 2
            sA, sc = (AFr, 5824) if par == 0 else (b.AACC, 0)
            src = sA[:, sc:sc + I_FF]

            def fsync(eng, ht=ht, src=src,
                      wv=(qv_n[ht - 2] if ht >= 2 else sec_v0)):
                eng.wait_ge(b.s_v.h, wv)
                b.s_ld.inc(eng.dma_start(out=src,
                                         in_=b.wd_in[ht * 128:(ht + 1) * 128, :]), 16)
                b.sentinel(eng)
            self.on("sync", fsync)
            ldv = b.s_ld.n

            def fvw(eng, ldv=ldv, ht=ht,
                    wt=(tv_n[ht - 1] if ht >= 1 else sec_t0)):
                eng.wait_ge(b.s_ld.h, ldv)
                eng.wait_ge(b.s_t.h, wt)
                if ht < 1:
                    eng.wait_ge(b.s_st.h, sec_st0)
            self.on("vector", fvw)
            for half in range(2):
                self.on("vector", self.emit_quant(
                    1408, sA[:, sc + half * 1408:sc + half * 1408 + 1408], tcol=0,
                    hi_ap=AQ[:, half * 1408:half * 1408 + 1408],
                    lo_ap=AQ[:, 2816 + half * 1408:2816 + half * 1408 + 1408]))

            def fv2(eng):
                b.s_v.inc(eng.last)
            self.on("vector", fv2)
            qv_n.append(b.s_v.n)
            vvq = b.s_v.n
            pairs = []
            for ihl in range(2):
                for s in range(22):
                    pairs.append((AQ[:, ihl * 2816 + s * 128:ihl * 2816 + (s + 1) * 128],
                                  b.AW[:, ihl * 2816 + s * 128:ihl * 2816 + (s + 1) * 128]))
            ws = st_n[ht - 1] if ht >= 1 else sec_st0
            self.emit_tr_blocks(
                pairs, pre_wait=lambda eng, vvq=vvq, ws=ws: (
                    eng.wait_ge(b.s_v.h, vvq), eng.wait_ge(b.s_st.h, ws)))
            tv_n.append(b.s_t.n)
            vv2 = b.s_v.n

            def fst(eng, ht=ht, vv2=vv2):
                eng.wait_ge(b.s_v.h, vv2)
                for ihl in range(2):
                    dst = b.wd_s[:, ihl, :, ht * 128:(ht + 1) * 128].transpose([1, 0, 2])
                    src = b.AW[:, ihl * 2816:ihl * 2816 + 2816].rearrange(
                        "p (k q) -> p k q", k=22)
                    b.s_st.inc(eng.dma_start(out=dst, in_=src), 16)
            self.on("gpsimd", fst)
            st_n.append(b.s_st.n)
        self.psum_close(2)
        self.barrier()

    def emit_bias_setup(self):
        b = self

        def fsync(eng):
            b.s_ld.inc(eng.dma_start(out=b.AFr[:, BV_COL:BV_COL + C], in_=b.bv_bc), 16)
            b.s_ld.inc(eng.dma_start(out=b.AFr[:, BP_COL:BP_COL + C], in_=b.bp_bc), 16)
        self.on("sync", fsync)

    def _a1(self, bi, c):
        b = self
        AFr, AT, AXNT, AACC, AW, TAB, AQ = b.AFr, b.AT, b.AXNT, b.AACC, b.AW, b.TAB, b.AQ
        self.trA = self.psum_open("trA", 128, BF16)
        self.trB = self.psum_open("trB", 128, BF16)
        if not hasattr(self, "_ln_q"):
            self._ln_q = []
            self._ln_st = []
        for j in range(4):
            t = 4 * c + j
            p = j % 2
            gi = len(self._ln_q)
            xb = AFr[:, 0:1024] if p == 0 else AFr[:, 4096:5120]
            xn = AFr[:, 1024:2048] if p == 0 else AFr[:, 5120:6144]
            aqb = 0 if p == 0 else 2048
            STB = TC_ST + p * 8

            def fsync(eng, t=t, xb=xb,
                      wq=(self._ln_q[gi - 2] if gi >= 2 else None)):
                eng.wait_ge(b.s_v.h, wq if wq is not None else b.s_v.n)
                b.s_ld.inc(eng.dma_start(out=xb,
                                         in_=b.x[bi, t * 128:(t + 1) * 128, :]), 16)
                b.sentinel(eng)
            self.on("sync", fsync)
            ldv = b.s_ld.n

            def fscl(eng, ldv=ldv, xb=xb, xn=xn, STB=STB,
                     ws=(self._ln_st[gi - 2] if gi >= 2 else None)):
                eng.wait_ge(b.s_ld.h, ldv)
                eng.wait_ge(b.s_st.h, ws if ws is not None else b.s_st.n)
                st = TAB[:, STB:STB + 1]
                st2 = TAB[:, STB + 1:STB + 2]
                eng.activation(out=xn, in_=xb, func=AF.Copy, accum_out=st)
                eng.activation(out=xn, in_=xb, func=AF.Square, accum_out=st2)
                b.s_s.inc(eng.last)
            self.on("scalar", fscl)
            sv = b.s_s.n

            def fvec(eng, sv=sv, STB=STB):
                eng.wait_ge(b.s_s.h, sv)
                st = TAB[:, STB:STB + 1]
                st2 = TAB[:, STB + 1:STB + 2]
                mu = TAB[:, STB + 2:STB + 3]
                var = TAB[:, STB + 3:STB + 4]
                eng.tensor_scalar(out=mu, in0=st, scalar1=1.0 / C, scalar2=None, op0=OP.mult)
                eng.tensor_scalar(out=var, in0=st2, scalar1=1.0 / C, scalar2=None, op0=OP.mult)
                eng.drain()
                eng.tensor_tensor(out=st, in0=mu, in1=mu, op=OP.mult)
                eng.drain()
                eng.tensor_tensor(out=var, in0=var, in1=st, op=OP.subtract)
                b.s_v.inc(eng.last)
            self.on("vector", fvec)
            vv = b.s_v.n

            def fscl2(eng, vv=vv, STB=STB):
                eng.wait_ge(b.s_v.h, vv)
                eng.activation(out=TAB[:, STB + 4:STB + 5],
                               in_=TAB[:, STB + 3:STB + 4], func=AF.Sqrt,
                               bias=np.float32(1e-5), scale=np.float32(1.0))
                b.s_s.inc(eng.last)
            self.on("scalar", fscl2)
            sv2 = b.s_s.n

            def fvec2(eng, sv2=sv2, xb=xb, xn=xn, aqb=aqb, STB=STB):
                eng.wait_ge(b.s_s.h, sv2)
                rs = TAB[:, STB + 5:STB + 6]
                mu = TAB[:, STB + 2:STB + 3]
                eng.reciprocal(out=rs, in_=TAB[:, STB + 4:STB + 5])
                eng.drain()
                eng.tensor_scalar(out=xn, in0=xb,
                                  scalar1=mu[:, 0:1], scalar2=rs[:, 0:1],
                                  op0=OP.subtract, op1=OP.mult)
                eng.tensor_copy(out=AQ[:, aqb:aqb + 1024], in_=xn)
                eng.tensor_tensor(out=AQ[:, aqb + 1024:aqb + 2048], in0=xn,
                                  in1=AQ[:, aqb:aqb + 1024], op=OP.subtract)
                b.s_v.inc(eng.last)
            self.on("vector", fvec2)
            vv2 = b.s_v.n
            self._ln_q.append(vv2)

            def fst(eng, t=t, vv2=vv2, xn=xn):
                eng.wait_ge(b.s_v.h, vv2)
                b.s_st.inc(eng.dma_start(out=b.xn_s[t * 128:(t + 1) * 128, :],
                                         in_=xn), 16)
            self.on("gpsimd", fst)
            self._ln_st.append(b.s_st.n)

            pairs = []
            for ihl in range(2):
                for kk in range(8):
                    pairs.append((AQ[:, aqb + ihl * 1024 + kk * 128:
                                     aqb + ihl * 1024 + (kk + 1) * 128],
                                  AXNT[:, ihl * 4096 + kk * 512 + j * 128:
                                       ihl * 4096 + kk * 512 + (j + 1) * 128]))
            self.emit_tr_blocks(
                pairs, pre_wait=lambda eng, vv2=vv2: eng.wait_ge(b.s_v.h, max(vv2, b.s_v.n)))
        vvx = b.s_v.n
        self.psum_close(2)
        pse = [self.psum_open("pe6", 512, F32), self.psum_open("pe7", 512, F32)]
        bank8 = b.ps + pse

        def xnT(ihl, kk):
            return b.AXNT[:, ihl * 4096 + kk * 512:ihl * 4096 + (kk + 1) * 512]
        for wave, colbase in A1_WAVES:
            tv_kk = []
            wave_t0 = b.s_t.n
            for kk in range(8):
                def fsync(eng, kk=kk, colbase=colbase,
                          tneed=None):
                    slot = kk % 2
                    depth = 2 if PIPE_A1 else 1
                    eng.wait_ge(b.s_t.h, tv_kk[kk - depth] if kk >= depth else wave_t0)
                    for ihl in range(2):
                        b.s_ld.inc(eng.dma_start(
                            out=AW[:, (slot * 2 + ihl) * 1024:(slot * 2 + ihl + 1) * 1024],
                            in_=b.wqkv_s[kk, ihl, :, colbase:colbase + C]), 16)
                self.on("sync", fsync)
                ldv = b.s_ld.n

                def ften(eng, kk=kk, wave=wave, ldv=ldv, vvx=vvx):
                    eng.wait_ge(b.s_ld.h, ldv)
                    eng.wait_ge(b.s_v.h, vvx)
                    if kk == 0:
                        eng.wait_ge(b.s_v.h, b.s_v.n)
                        eng.wait_ge(b.s_s.h, b.s_s.n)
                    slot = kk % 2

                    def wbuf(ihl):
                        return AW[:, (slot * 2 + ihl) * 1024:(slot * 2 + ihl + 1) * 1024]
                    last = None
                    if wave in ("q", "k"):
                        for g in range(NG):
                            for pi, (whl, xhl) in enumerate(PASSES3):
                                last = eng.matmul(out=bank8[g][:, :],
                                                  lhsT=wbuf(whl)[:, g * 128:(g + 1) * 128],
                                                  rhs=xnT(xhl, kk),
                                                  start=(kk == 0 and pi == 0),
                                                  stop=(kk == 7 and pi == 2))
                    else:
                        for j in range(4):
                            for pi, (whl, xhl) in enumerate(PASSES3):
                                for half in range(2):
                                    last = eng.matmul(
                                        out=bank8[j * 2 + half][:, :],
                                        lhsT=xnT(xhl, kk)[:, j * 128:(j + 1) * 128],
                                        rhs=wbuf(whl)[:, half * 512:(half + 1) * 512],
                                        start=(kk == 0 and pi == 0),
                                        stop=(kk == 7 and pi == 2))
                    b.s_t.inc(last)
                self.on("tensor", ften)
                tv_kk.append(b.s_t.n)
            tvw = b.s_t.n
            if wave in ("q", "k"):
                for g in range(NG):
                    def fscl(eng, g=g, wave=wave, tvw=tvw):
                        eng.wait_ge(b.s_t.h, tvw)
                        eng.wait_ge(b.s_v.h, b.s_v.n)
                        col = (TC_BQ if wave == "q" else TC_BK) + g
                        eng.activation(out=AFr[:, 2048:2560], in_=bank8[g][:, :],
                                       func=AF.Identity, bias=TAB[:, col:col + 1],
                                       scale=np.float32(1.0))
                        b.s_s.inc(eng.last)
                    self.on("scalar", fscl)
                    sv = b.s_s.n

                    def fvec(eng, g=g, c=c, wave=wave, sv=sv):
                        eng.wait_ge(b.s_s.h, sv)
                        q_ = AFr[:, 2048:2560]
                        sw = AFr[:, 2560:3072]
                        t1 = AFr[:, 3072:3584]
                        r32 = AFr[:, 3584:4096]
                        eng.stream_shuffle(out=sw, in_=q_, mask=SWAP_MASK)
                        eng.tensor_tensor(out=t1, in0=q_,
                                          in1=TAB[:, TC_COS + 512 * c:TC_COS + 512 * (c + 1)],
                                          op=OP.mult)
                        eng.tensor_tensor(out=sw, in0=sw,
                                          in1=TAB[:, TC_SIN + 512 * c:TC_SIN + 512 * (c + 1)],
                                          op=OP.mult)
                        eng.tensor_tensor(out=r32, in0=t1, in1=sw, op=OP.add)
                        base = 0 if wave == "q" else 8192
                        hi = AT[:, base + g * 512:base + (g + 1) * 512]
                        lo = AT[:, base + 4096 + g * 512:base + 4096 + (g + 1) * 512]
                        eng.tensor_copy(out=hi, in_=r32)
                        eng.tensor_tensor(out=lo, in0=r32, in1=hi, op=OP.subtract)
                        b.s_v.inc(eng.last)
                    self.on("vector", fvec)
                if wave == "k":
                    vvk = b.s_v.n

                    def fstk(eng, c=c, vvk=vvk):
                        eng.wait_ge(b.s_v.h, vvk)
                        for g in range(NG):
                            for ihl, base in [(0, 8192), (1, 8192 + 4096)]:
                                b.s_st.inc(eng.dma_start(
                                    out=b.kt_s[g, ihl, :, c * 512:(c + 1) * 512],
                                    in_=AT[:, base + g * 512:base + (g + 1) * 512]), 16)
                    self.on("gpsimd", fstk)
            else:
                for j in range(4):
                    def fvec(eng, j=j, tvw=tvw):
                        eng.wait_ge(b.s_t.h, tvw)
                        for half in range(2):
                            eng.tensor_tensor(
                                out=AACC[:, j * 1024 + half * 512:j * 1024 + (half + 1) * 512],
                                in0=bank8[j * 2 + half][:, :],
                                in1=AFr[:, BV_COL + half * 512:BV_COL + (half + 1) * 512],
                                op=OP.add)
                        b.s_v.inc(eng.last)
                    self.on("vector", fvec)
                vvv = b.s_v.n

                def fstv(eng, c=c, vvv=vvv):
                    eng.wait_ge(b.s_v.h, vvv)
                    dst = b.v_s[c * 512:(c + 1) * 512, :].rearrange(
                        "(j p) f -> j p f", p=128).transpose([1, 0, 2])
                    src = AACC[:, 0:4096].rearrange("p (j f) -> p j f", j=4)
                    b.s_st.inc(eng.dma_start(out=dst, in_=src), 16)
                self.on("gpsimd", fstv)
        self.psum_close(2)

    def _a2(self, bi, c):
        b = self
        AFr, AT, AXNT, TAB, AQ = b.AFr, b.AT, b.AXNT, b.TAB, b.AQ
        nr = 4 * c + 4
        pbc = self.psum_open("pbc", 512, F32)
        PT = [2048, 5632]

        def emit_av(g, r, tv_av, sv_r, vprev):
            rb = r % 2
            pb = PT[rb]
            sve = sv_r[r]

            def ften_av(eng, g=g, r=r, rb=rb, pb=pb, sve=sve,
                        first=(r == 0), last=(r == nr - 1), vprev=vprev):
                eng.wait_ge(b.s_s.h, sve)
                if first:
                    eng.wait_ge(b.s_v.h, vprev)
                vb = b.VBUF[:, rb * 128:(rb + 1) * 128]
                eng.matmul(out=b.ps[4][0:64, :], lhsT=vb[:, 0:64],
                           rhs=AFr[:, pb:pb + 512], start=first, stop=last,
                           tile_position=(0, 0), skip_group_check=True)
                eng.matmul(out=b.ps[4][64:128, :], lhsT=vb[:, 64:128],
                           rhs=AFr[:, pb + 512:pb + 1024], start=first, stop=last,
                           tile_position=(0, 64), skip_group_check=True)
                eng.matmul(out=b.ps[5][0:1, :], lhsT=b.ONESF[:, 0:1],
                           rhs=AFr[:, pb:pb + 512], start=first, stop=last,
                           tile_position=(0, 0), skip_group_check=True)
                b.s_t.inc(eng.matmul(out=b.ps[5][32:33, :], lhsT=b.ONESF[:, 1:2],
                           rhs=AFr[:, pb + 512:pb + 1024], start=first, stop=last,
                           tile_position=(0, 32), skip_group_check=True))
            self.on("tensor", ften_av)
            tv_av.append(b.s_t.n)

        for g in range(NG):
            if g == 1:
                self._a3_prefetch(bi, c)
            tv_av = []
            tv_s = []
            sv_r = []
            vprev = b.s_v.n
            for r in range(nr):
                rb = r % 2
                sb_bank = 2 * rb

                def fsync_r(eng, g=g, r=r, rb=rb):
                    eng.wait_ge(b.s_t.h, tv_av[r - 2] if r >= 2 else b.s_t.n)
                    for ihl in range(2):
                        b.s_ld.inc(eng.dma_start(
                            out=b.KBUF[:, rb * 512 + ihl * 128:rb * 512 + (ihl + 1) * 128],
                            in_=b.kt_s[g, ihl, :, r * 128:(r + 1) * 128]), 16)
                    b.s_ld.inc(eng.dma_start(
                        out=b.VBUF[:, rb * 128:(rb + 1) * 128],
                        in_=b.v_s[r * 128:(r + 1) * 128, 128 * g:128 * (g + 1)]), 16)
                self.on("sync", fsync_r)
                ldv = b.s_ld.n

                def ften_s(eng, g=g, r=r, rb=rb, sb_bank=sb_bank, ldv=ldv):
                    eng.wait_ge(b.s_ld.h, ldv)
                    if r >= 2:
                        eng.wait_ge(b.s_s.h, sv_r[r - 2])

                    def kb(ihl):
                        return b.KBUF[:, rb * 512 + ihl * 128:rb * 512 + (ihl + 1) * 128]
                    last = None
                    for pi, (khl, qhl) in enumerate(PASSES3):
                        qsl = AT[:, qhl * 4096 + g * 512:qhl * 4096 + (g + 1) * 512]
                        eng.matmul(out=b.ps[sb_bank][:, :], lhsT=kb(khl)[0:64, :],
                                   rhs=qsl[0:64, :], start=(pi == 0), stop=(pi == 2),
                                   tile_position=(0, 0))
                        last = eng.matmul(out=b.ps[sb_bank + 1][:, :], lhsT=kb(khl)[64:128, :],
                                          rhs=qsl[64:128, :], start=(pi == 0), stop=(pi == 2),
                                          tile_position=(64, 0))
                    b.s_t.inc(last)
                self.on("tensor", ften_s)
                tv_s.append(b.s_t.n)
                tvs = b.s_t.n

                jj = r - 4 * c
                mv = None
                if jj >= 0:
                    def fvec_m(eng, jj=jj, sb_bank=sb_bank, tvs=tvs):
                        eng.wait_ge(b.s_t.h, tvs)
                        w = (jj + 1) * 128
                        for bk in range(2):
                            eng.tensor_tensor(out=b.ps[sb_bank + bk][:, 0:w],
                                              in0=b.ps[sb_bank + bk][:, 0:w],
                                              in1=b.MSK[:, jj * 512:jj * 512 + w],
                                              op=OP.add)
                        b.s_v.inc(eng.last)
                    self.on("vector", fvec_m)
                    mv = b.s_v.n

                def fscl_e(eng, r=r, rb=rb, sb_bank=sb_bank, tvs=tvs, mv=mv):
                    eng.wait_ge(b.s_t.h, tvs)
                    if r >= 2:
                        eng.wait_ge(b.s_t.h, tv_av[r - 2])
                    if mv is not None:
                        eng.wait_ge(b.s_v.h, mv)
                    pb = PT[rb]
                    for bk in range(2):
                        eng.activation(out=AFr[:, pb + bk * 512:pb + (bk + 1) * 512],
                                       in_=b.ps[sb_bank + bk][:, :], func=AF.Exp,
                                       bias=np.float32(-4.0), scale=np.float32(0.125))
                    b.s_s.inc(eng.last)
                self.on("scalar", fscl_e)
                sv_r.append(b.s_s.n)
                if PIPE_A2:
                    if r >= 1:
                        emit_av(g, r - 1, tv_av, sv_r, vprev)
                else:
                    emit_av(g, r, tv_av, sv_r, vprev)
            if PIPE_A2:
                emit_av(g, nr - 1, tv_av, sv_r, vprev)
            tva = tv_av[-1]

            def fvec_d(eng, tva=tva):
                eng.wait_ge(b.s_t.h, tva)
                dn = TAB[:, TC_DEN:TC_DEN + 512]
                eng.tensor_copy(out=dn[0:1, :], in_=b.ps[5][0:1, :])
                eng.tensor_copy(out=dn[32:33, :], in_=b.ps[5][32:33, :])
                eng.reciprocal(out=dn[0:1, :], in_=dn[0:1, :])
                eng.reciprocal(out=dn[32:33, :], in_=dn[32:33, :])
                b.s_v.inc(eng.last)
            self.on("vector", fvec_d)
            vvd = b.s_v.n

            def ften_b(eng, vvd=vvd):
                eng.wait_ge(b.s_v.h, vvd)
                dn = TAB[:, TC_DEN:TC_DEN + 512]
                eng.matmul(out=pbc[0:64, :], lhsT=b.ONESF[0:1, 0:64],
                           rhs=dn[0:1, :], start=True, stop=True, tile_position=(0, 0))
                b.s_t.inc(eng.matmul(out=pbc[64:128, :], lhsT=b.ONESF[32:33, 0:64],
                           rhs=dn[32:33, :], start=True, stop=True,
                           tile_position=(32, 64), skip_group_check=True))
            self.on("tensor", ften_b)
            tvb = b.s_t.n

            def fvec_y(eng, g=g, tvb=tvb):
                eng.wait_ge(b.s_t.h, tvb)
                bc = AFr[:, 3072:3584]
                yn = AFr[:, 3584:4096]
                eng.tensor_copy(out=bc, in_=pbc[:, :])
                eng.tensor_tensor(out=yn, in0=b.ps[4][:, :], in1=bc, op=OP.mult)
                hi = AXNT[:, g * 512:(g + 1) * 512]
                lo = AXNT[:, 4096 + g * 512:4096 + (g + 1) * 512]
                eng.tensor_copy(out=hi, in_=yn)
                eng.tensor_tensor(out=lo, in0=yn, in1=hi, op=OP.subtract)
                b.s_v.inc(eng.last)
            self.on("vector", fvec_y)
        self.psum_close(1)

    def _a3_prefetch(self, bi, c):
        b = self

        def fsync0(eng, bi=bi, c=c):
            dst = b.AACC[:, 0:4096].rearrange("p (j f) -> p j f", j=4)
            src = b.x[bi, c * 512:(c + 1) * 512, :].rearrange(
                "(j p) f -> j p f", p=128).transpose([1, 0, 2])
            b.s_ld.inc(eng.dma_start(out=dst, in_=src), 16)
            b.sentinel(eng)
        self.on("sync", fsync0)
        self._a3_ldx = b.s_ld.n

    def _a3(self, bi, c):
        b = self
        AFr, AXNT, AACC, AW = b.AFr, b.AXNT, b.AACC, b.AW
        ldx = self._a3_ldx

        def fvec_x(eng, ldx=ldx):
            eng.wait_ge(b.s_ld.h, ldx)
            for j in range(4):
                eng.tensor_tensor(out=AACC[:, j * 1024:(j + 1) * 1024],
                                  in0=AACC[:, j * 1024:(j + 1) * 1024],
                                  in1=AFr[:, BP_COL:BP_COL + C], op=OP.add)
            b.s_v.inc(eng.last)
        self.on("vector", fvec_x)
        vvx = b.s_v.n
        pse = [self.psum_open("pa6", 512, F32), self.psum_open("pa7", 512, F32)]
        bank8 = b.ps + pse
        tv_g = []
        a3_t0 = b.s_t.n
        for g in range(NG):
            def fsync(eng, g=g):
                slot = g % 2
                depth = 2 if PIPE_A3 else 1
                eng.wait_ge(b.s_t.h, tv_g[g - depth] if g >= depth else a3_t0)
                for ihl in range(2):
                    b.s_ld.inc(eng.dma_start(
                        out=AW[:, (slot * 2 + ihl) * 1024:(slot * 2 + ihl + 1) * 1024],
                        in_=b.wproj_s[g, ihl, :, :]), 16)
            self.on("sync", fsync)
            ldv = b.s_ld.n

            def ften(eng, g=g, ldv=ldv, vvx=vvx):
                eng.wait_ge(b.s_ld.h, ldv)
                if g == 0:
                    eng.wait_ge(b.s_v.h, max(vvx, b.s_v.n))
                slot = g % 2
                last = None
                for j in range(4):
                    for half in range(2):
                        for pi, (whl, yhl) in enumerate(PASSES3):
                            last = eng.matmul(
                                out=bank8[j * 2 + half][:, :],
                                lhsT=AXNT[:, yhl * 4096 + g * 512 + j * 128:
                                          yhl * 4096 + g * 512 + (j + 1) * 128],
                                rhs=AW[:, (slot * 2 + whl) * 1024 + half * 512:
                                       (slot * 2 + whl) * 1024 + (half + 1) * 512],
                                start=(g == 0 and pi == 0),
                                stop=(g == NG - 1 and pi == 2))
                b.s_t.inc(last)
            self.on("tensor", ften)
            tv_g.append(b.s_t.n)
        tvp = b.s_t.n
        for j in range(4):
            def fvec(eng, j=j, tvp=tvp):
                eng.wait_ge(b.s_t.h, tvp)
                for half in range(2):
                    eng.tensor_tensor(
                        out=AACC[:, 4096 + j * 1024 + half * 512:
                                 4096 + j * 1024 + (half + 1) * 512],
                        in0=bank8[j * 2 + half][:, :],
                        in1=AACC[:, j * 1024 + half * 512:j * 1024 + (half + 1) * 512],
                        op=OP.add)
                b.s_v.inc(eng.last)
            self.on("vector", fvec)
        vvp = b.s_v.n

        def fst(eng, c=c, vvp=vvp):
            eng.wait_ge(b.s_v.h, vvp)
            dst = b.part_s[c * 512:(c + 1) * 512, :].rearrange(
                "(j p) f -> j p f", p=128).transpose([1, 0, 2])
            src = AACC[:, 4096:8192].rearrange("p (j f) -> p j f", j=4)
            b.s_st.inc(eng.dma_start(out=dst, in_=src), 16)
        self.on("gpsimd", fst)
        self.psum_close(2)

    def emit_phase_b(self, bi):
        b = self
        AFr, AT, AXNT, AACC, AW, AQ = b.AFr, b.AT, b.AXNT, b.AACC, b.AW, b.AQ
        self.trA = self.psum_open("trA", 128, BF16)
        self.trB = self.psum_open("trB", 128, BF16)
        for half in range(2):
            toff = half * 8

            def fsync_p(eng, toff=toff):
                dst = AACC[:, 0:8192].rearrange("p (j f) -> p j f", j=8)
                src = b.part_s[toff * 128:(toff + 8) * 128, :].rearrange(
                    "(j p) f -> j p f", p=128).transpose([1, 0, 2])
                b.s_ld.inc(eng.dma_start(out=dst, in_=src), 16)
                b.sentinel(eng)
            self.on("sync", fsync_p)
            ldp = b.s_ld.n
            for tj in range(8):
                t = toff + tj

                def fsync(eng, t=t):
                    eng.wait_ge(b.s_v.h, b.s_v.n)
                    b.s_ld.inc(eng.dma_start(out=AFr[:, 0:1024],
                                             in_=b.xn_s[t * 128:(t + 1) * 128, :]), 16)
                    b.sentinel(eng)
                self.on("sync", fsync)
                ldv = b.s_ld.n

                def fvw(eng, ldv=ldv):
                    eng.wait_ge(b.s_ld.h, ldv)
                self.on("vector", fvw)
                self.on("vector", self.emit_quant(C, AFr[:, 0:1024], tcol=1024,
                                                  hi_ap=AQ[:, 0:1024],
                                                  lo_ap=AQ[:, 1024:2048]))

                def fv2(eng):
                    b.s_v.inc(eng.last)
                self.on("vector", fv2)
                vv = b.s_v.n

                pairs = []
                for ihl in range(2):
                    for kk in range(8):
                        pairs.append((AQ[:, ihl * 1024 + kk * 128:ihl * 1024 + (kk + 1) * 128],
                                      AT[:, ihl * 8192 + kk * 1024 + tj * 128:
                                           ihl * 8192 + kk * 1024 + (tj + 1) * 128]))
                self.emit_tr_blocks(pairs,
                                    pre_wait=lambda eng, vv=vv: eng.wait_ge(b.s_v.h, max(vv, b.s_v.n)))
            self.barrier()
            icol = 0
            n_tile = 0
            h_done = []
            pend = None
            GU = [(0, 1), (4, 5)]

            def emit_tail(pend, facc_only=False):
                (tj, IWp, nsubp, wd_bp, vq, ldw_p) = pend
                par = (n_tile - 1) % 2
                pairs = []
                for ihl in range(2):
                    for s in range(nsubp):
                        pairs.append((AQ[:, ihl * 512 + s * 128:ihl * 512 + (s + 1) * 128],
                                      AXNT[:, (ihl * nsubp + s) * 128:
                                           (ihl * nsubp + s + 1) * 128]))
                self.emit_tr_blocks(pairs,
                                    pre_wait=lambda eng, vq=vq: eng.wait_ge(b.s_v.h, vq))
                vve = b.s_v.n

                def hqT(ihl, s, nsubp=nsubp):
                    return AXNT[:, (ihl * nsubp + s) * 128:(ihl * nsubp + s + 1) * 128]

                def ften_d(eng, nsubp=nsubp, vve=vve, wd_bp=wd_bp, ldw_p=ldw_p):
                    eng.wait_ge(b.s_v.h, vve)
                    eng.wait_ge(b.s_ld.h, ldw_p)
                    last = None
                    for s in range(nsubp):
                        for pi, (hhl, whl) in enumerate(PASSES3):
                            for ch in range(2):
                                last = eng.matmul(out=b.ps[2 + ch][:, :],
                                           lhsT=hqT(hhl, s),
                                           rhs=wd_bp(s, whl)[:, ch * 512:(ch + 1) * 512],
                                           start=(s == 0 and pi == 0),
                                           stop=(s == nsubp - 1 and pi == 2))
                    b.s_t.inc(last)
                self.on("tensor", ften_d)
                tvd = b.s_t.n

                def facc(eng, tj=tj, tvd=tvd):
                    eng.wait_ge(b.s_t.h, tvd)
                    for ch in range(2):
                        a = AACC[:, tj * 1024 + ch * 512:tj * 1024 + (ch + 1) * 512]
                        eng.tensor_tensor(out=a, in0=a, in1=b.ps[2 + ch][:, :],
                                          op=OP.add)
                    b.s_v.inc(eng.last)
                self.on("vector", facc)
                return b.s_v.n

            prev_tv_gu = None
            for ic, IW in enumerate(I_CHUNKS):
                nsub = IW // 128
                if pend is not None:
                    emit_tail(pend)
                    pend = None

                def fsync1(eng, icol=icol, IW=IW,
                           gate=prev_tv_gu):
                    eng.wait_ge(b.s_t.h, gate if gate is not None else b.s_t.n)
                    for kh in range(2):
                        for wi, w_s in [(0, b.wg_s), (1, b.wu_s)]:
                            for a in range(2):
                                src_ = w_s[a, kh * 4:kh * 4 + 4, :,
                                           icol:icol + IW].transpose([1, 0, 2])
                                base = wi * 4096 + a * 8 * IW + kh * 4 * IW
                                dst = AW[:, base:base + 4 * IW].rearrange(
                                    "p (k q) -> p k q", k=4)
                                b.s_ld.inc(eng.dma_start(out=dst, in_=src_), 16)
                        if kh == 0:
                            ld_half[0] = b.s_ld.n
                ld_half = [0]
                self.on("sync", fsync1)
                ldv = b.s_ld.n
                ldv1 = ld_half[0]

                def fsync2(eng, icol=icol, nsub=nsub):
                    eng.wait_ge(b.s_t.h, b.s_t.n)
                    sub0 = icol // 128
                    src_ = b.wd_s[sub0:sub0 + nsub, :, :, :].transpose([2, 0, 1, 3])
                    dst = AW[:, 8192:8192 + nsub * 2048].rearrange(
                        "p (s a q) -> p s a q", s=nsub, a=2)
                    b.s_ld.inc(eng.dma_start(out=dst, in_=src_), 16)
                self.on("sync", fsync2)
                ldw = b.s_ld.n

                def wgu_b(wi, ihl, kk, IW=IW):
                    off = wi * 4096 + (ihl * 8 + kk) * IW
                    return AW[:, off:off + IW]

                def wd_b(s, ihl):
                    off = 8192 + (s * 2 + ihl) * 1024
                    return AW[:, off:off + 1024]
                for tj in range(8):
                    par = n_tile % 2
                    g0, g1 = GU[par]
                    sil = par * 512
                    hcol = 1024 + par * 256

                    def ften_gu(eng, tj=tj, IW=IW, ldv=ldv, ldv1=ldv1, ldp=ldp,
                                g0=g0, g1=g1,
                                hfree=(h_done[n_tile - 2] if n_tile >= 2 else 0)):
                        eng.wait_ge(b.s_ld.h, max(ldv1, ldp))
                        eng.wait_ge(b.s_v.h, hfree)
                        last = None
                        for kk in range(8):
                            if kk == 4:
                                eng.wait_ge(b.s_ld.h, ldv)
                            for pi, (whl, xhl) in enumerate(PASSES3):
                                lhs = AT[:, xhl * 8192 + kk * 1024 + tj * 128:
                                         xhl * 8192 + kk * 1024 + (tj + 1) * 128]
                                eng.matmul(out=b.ps[g0][:, 0:IW], lhsT=lhs,
                                           rhs=wgu_b(0, whl, kk),
                                           start=(kk == 0 and pi == 0),
                                           stop=(kk == 7 and pi == 2))
                                last = eng.matmul(out=b.ps[g1][:, 0:IW], lhsT=lhs,
                                                  rhs=wgu_b(1, whl, kk),
                                                  start=(kk == 0 and pi == 0),
                                                  stop=(kk == 7 and pi == 2))
                        b.s_t.inc(last)
                    self.on("tensor", ften_gu)
                    tv = b.s_t.n

                    def fscl(eng, IW=IW, tv=tv, g0=g0, sil=sil):
                        eng.wait_ge(b.s_t.h, tv)
                        eng.activation(out=AFr[:, sil:sil + IW], in_=b.ps[g0][:, 0:IW],
                                       func=AF.Silu, bias=np.float32(0.0),
                                       scale=np.float32(1.0))
                        b.s_s.inc(eng.last)
                    self.on("scalar", fscl)
                    sv = b.s_s.n

                    if pend is not None:
                        emit_tail(pend)
                        pend = None

                    def fvec_h(eng, IW=IW, sv=sv, g1=g1, sil=sil, hcol=hcol):
                        eng.wait_ge(b.s_s.h, sv)
                        eng.tensor_tensor(out=AFr[:, hcol:hcol + IW],
                                          in0=AFr[:, sil:sil + IW],
                                          in1=b.ps[g1][:, 0:IW], op=OP.mult)
                    self.on("vector", fvec_h)
                    self.on("vector", self.emit_quant(IW, AFr[:, hcol:hcol + IW],
                                                      tcol=2048,
                                                      hi_ap=AQ[:, 0:IW],
                                                      lo_ap=AQ[:, 512:512 + IW]))

                    def fv2(eng):
                        b.s_v.inc(eng.last)
                    self.on("vector", fv2)
                    h_done.append(b.s_v.n)
                    n_tile += 1
                    pend = (tj, IW, nsub, wd_b, b.s_v.n, ldw)
                    if (not PIPE_B) or ic == len(I_CHUNKS) - 1:
                        vvf = emit_tail(pend)
                        pend = None
                    if ic == len(I_CHUNKS) - 1:

                        def fout(eng, bi=bi, toff=toff, tj=tj, vvf=vvf):
                            eng.wait_ge(b.s_v.h, vvf)
                            b.s_st.inc(eng.dma_start(
                                out=b.out[bi, (toff + tj) * 128:(toff + tj + 1) * 128, :],
                                in_=AACC[:, tj * 1024:(tj + 1) * 1024]), 16)
                        self.on("gpsimd", fout)
                prev_tv_gu = tv
                icol += IW
            self.barrier()
        self.psum_close(2)

    def emit_scratch_dump(self):
        b = self

        def fd(eng):
            eng.wait_ge(b.s_v.h, b.s_v.n)
            eng.wait_ge(b.s_s.h, b.s_s.n)
            eng.wait_ge(b.s_t.h, b.s_t.n)
            eng.wait_ge(b.s_st.h, b.s_st.n)
            b.s_st.inc(eng.dma_start(out=b.dbg2[0], in_=b.xn_s[:, :]), 16)
            b.s_st.inc(eng.dma_start(out=b.dbg2[1], in_=b.part_s[:, :]), 16)
            b.s_st.inc(eng.dma_start(out=b.dbg2[2], in_=b.v_s[:, :]), 16)
            b.s_st.inc(eng.dma_start(out=b.dbg2[3, 0:128, :],
                                     in_=b.kt_s[0, 0, :, :].bitcast(F32)), 16)
        self.on("gpsimd", fd)

    def emit_debug_dump(self):
        b = self
        def fd(eng):
            eng.wait_ge(b.s_v.h, b.s_v.n)
            eng.wait_ge(b.s_s.h, b.s_s.n)
            eng.wait_ge(b.s_t.h, b.s_t.n)
            b.s_st.inc(eng.dma_start(out=b.dbg[0, :, 0:8192],
                                     in_=b.AT.ap().bitcast(F32)), 16)
            b.s_st.inc(eng.dma_start(out=b.dbg[1, :, 0:4096],
                                     in_=b.AXNT.ap().bitcast(F32)), 16)
            b.s_st.inc(eng.dma_start(out=b.dbg[2, :, 0:8192], in_=b.AFr[:, 0:8192]), 16)
        self.on("gpsimd", fd)

    def finish(self):
        b = self
        stv = b.s_st.n

        def fdone(eng, stv=stv):
            eng.wait_ge(b.s_st.h, stv)
        self.on("gpsimd", fdone)
        self.stack.close()
        return self.nc


def build_nc(stage="full"):
    b = Builder()
    b.emit_setup()
    if stage != "setup":
        b.emit_weight_prep()
        b.emit_bias_setup()
        b.barrier()
    if stage in ("full", "a1", "a2", "a3", "b1", "a_all", "ab1", "f2a"):
        nb = 2 if stage in ("full", "f2a") else 1
        nch = NCH if stage in ("full", "a_all", "ab1", "f2a") else 1
        for bi in range(nb):
            if stage != "b1":
                for c in range(nch):
                    b._a1(bi, c)
                    b.barrier()
                    if stage == "a1":
                        b.emit_debug_dump()
                        break
                    b._a2(bi, c)
                    b.barrier()
                    if stage == "a2":
                        b.emit_debug_dump()
                        break
                    b._a3(bi, c)
                    b.barrier()
            if stage in ("a_all",):
                b.emit_scratch_dump()
            if stage in ("full", "b1", "ab1") or (stage == "f2a" and bi == 0):
                b.emit_phase_b(bi)
                b.barrier()
                if stage == "ab1":
                    b.emit_scratch_dump()
    if stage == "full" and DEBUG:
        b.emit_scratch_dump()
    return b.finish()


def kernel(L0_attn_x, L0_attn_W_qkv, L0_attn_b_qkv, L0_attn_W_proj, L0_attn_b_proj,
           L0_ffn_w_gate, L0_ffn_w_up, L0_ffn_w_down):
    x = np.ascontiguousarray(np.asarray(L0_attn_x, dtype=np.float32))
    tabs = _host_tables()
    btabs = _host_bias_tables(np.asarray(L0_attn_b_qkv, dtype=np.float32),
                              np.asarray(L0_attn_b_proj, dtype=np.float32))
    nc = build_nc()
    base = {
        "wqkv": np.ascontiguousarray(np.asarray(L0_attn_W_qkv, dtype=np.float32)),
        "wproj": np.ascontiguousarray(np.asarray(L0_attn_W_proj, dtype=np.float32)),
        "wg_in": np.ascontiguousarray(np.asarray(L0_ffn_w_gate, dtype=np.float32)),
        "wu_in": np.ascontiguousarray(np.asarray(L0_ffn_w_up, dtype=np.float32)),
        "wd_in": np.ascontiguousarray(np.asarray(L0_ffn_w_down, dtype=np.float32)),
        **tabs, **btabs,
    }
    in_maps = []
    for core in range(8):
        m = dict(base)
        m["x"] = np.ascontiguousarray(x[2 * core:2 * core + 2])
        in_maps.append(m)
    res = run_bass_kernel_spmd(nc, in_maps, core_ids=list(range(8)))
    out = np.concatenate([np.asarray(r["out"]) for r in res.results], axis=0)
    return out.astype(np.float32)
